# version 27
# baseline (speedup 1.0000x reference)
from contextlib import ExitStack

import numpy as np
import ml_dtypes
import concourse.bass as bass
import concourse.mybir as mybir
from concourse.bass_utils import run_bass_kernel_spmd

F32 = mybir.dt.float32
BF16 = mybir.dt.bfloat16
I32 = mybir.dt.int32
AF = mybir.ActivationFunctionType
ALU = mybir.AluOpType
BF = ml_dtypes.bfloat16

T = 2048
D = 1024
NH = 8
DQK = 128
DV = 256
VDIM = 2048
HB = 16
FH = 3072
CK = 128
EPS = 1e-5
SC_A = 0.088728561997413635
SC_B = 0.088742740452289581
N_SUB = 4
DEV_PHASES = ("qk", "v", "g", "mx", "wo")
DEV_NQ = 4
DEV_QK_LVL = 6
DEV_TAP = False

ENGS = ("tensor", "scalar", "vector", "gpsimd", "sync")


def _ref_f32_consts():
    try:
        import jax
        import jax.numpy as jnp
        cpu = jax.local_devices(backend="cpu")[0]
        with jax.default_device(cpu):
            inv = 1.0 / 10000.0 ** (jnp.arange(0, DQK, 2, dtype=jnp.float32) / DQK)
            ang = jnp.arange(T, dtype=jnp.float32)[:, None] * inv[None, :]
            cos = np.asarray(jnp.cos(ang))
            sin = np.asarray(jnp.sin(ang))
            s32 = np.asarray(jnp.log2(1.0 - jnp.exp2(-5.0 - jnp.arange(NH, dtype=jnp.float32))))
            dd = (jnp.arange(CK, dtype=jnp.float32)[None, :]
                  - jnp.arange(CK, dtype=jnp.float32)[:, None])
            pair = np.stack([np.asarray(jnp.exp2(dd * jnp.float32(s32[h])))
                             for h in range(NH)], axis=0)
        return cos, sin, s32, pair
    except Exception:
        pass
    try:
        raise RuntimeError()
    except Exception:
        inv = (np.float32(1.0) /
               np.float32(10000.0) ** (np.arange(0, DQK, 2, dtype=np.float32) / np.float32(DQK)))
        ang = (np.arange(T, dtype=np.float32)[:, None] * inv[None, :]).astype(np.float32)
        s32f = np.log2(np.float32(1.0) - np.exp2(np.float32(-5.0)
                                                 - np.arange(NH, dtype=np.float32))).astype(np.float32)
        ddf = (np.arange(CK, dtype=np.float32)[None, :]
               - np.arange(CK, dtype=np.float32)[:, None]).astype(np.float32)
        pairf = np.stack([np.exp2((ddf * s32f[h]).astype(np.float32)).astype(np.float32)
                          for h in range(NH)], axis=0)
        return np.cos(ang).astype(np.float32), np.sin(ang).astype(np.float32), s32f, pairf


def _host_tables():
    t = {}
    cos, sin, s32, pair = _ref_f32_consts()
    cos64 = cos.astype(np.float64).T
    sin64 = sin.astype(np.float64).T
    C = np.zeros((DQK, T)); S = np.zeros((DQK, T))
    C[0::2, :] = cos64; C[1::2, :] = cos64
    S[0::2, :] = -sin64; S[1::2, :] = sin64
    qs = DQK ** (-0.5)
    t["ropeCq"] = (C * qs).astype(np.float32)
    t["ropeSq"] = (S * qs).astype(np.float32)
    t["ropeCk"] = C.astype(np.float32)
    t["ropeSk"] = S.astype(np.float32)
    s64 = s32.astype(np.float64)
    idx = np.arange(CK)
    lamT = np.zeros((CK, NH, CK), np.float32)
    gr = np.zeros((CK, NH, CK)); kd = np.zeros((CK, NH))
    dd = (idx[None, :] - idx[:, None]).astype(np.float32)
    for h in range(NH):
        lamT[:, h, :] = np.where(dd >= 0, pair[h], np.float32(0.0))
        gr[:, h, :] = 2.0 ** (s64[h] * idx[None, :])
        kd[:, h] = 2.0 ** (s64[h] * (CK - idx))
    t["lamT"] = lamT.reshape(CK, NH * CK).astype(np.float32)
    t["gamR"] = gr.reshape(CK, NH * CK).astype(np.float32)
    t["kdec"] = kd.astype(np.float32)
    t["gamC"] = [float(2.0 ** (s64[h] * CK)) for h in range(NH)]
    P = np.zeros((128, 128), np.float32)
    for m in range(128):
        P[m ^ 1, m] = 1.0
    t["perm"] = P.astype(BF)
    t["ident"] = np.eye(128, dtype=np.float32).astype(BF)
    t["inv1024"] = np.full((128, 128), 1.0 / 1024.0, np.float32).astype(BF)
    t["inv256"] = np.full((128, 128), 1.0 / 256.0, np.float32).astype(BF)
    t["ones65"] = np.ones((65, 64), np.float32).astype(BF)
    mA = np.ones((128, 512), np.float32); mA[127, 0] = 0.0
    mB = np.ones((128, 512), np.float32); mB[0, 511] = 0.0
    t["maskA"] = mA.astype(BF)
    t["maskB"] = mB.astype(BF)
    t["epsc"] = np.full((128, 1), EPS, np.float32)
    return t


class Sem:
    __slots__ = ("h", "n", "name")

    def __init__(self, h, name):
        self.h, self.n, self.name = h, 0, name


class Res:
    __slots__ = ("w", "r")

    def __init__(self, seed=()):
        self.w, self.r = None, list(seed)

    def wdeps(self):
        d = list(self.r)
        if self.w is not None:
            d.append(self.w)
        return d

    def wrote(self, tok):
        self.w, self.r = tok, []

    def read(self, tok):
        self.r.append(tok)


class PB:
    def __init__(self, nc):
        self.nc = nc
        self.ops = []
        self.all_sems = []
        self.esem = {}
        for e in ENGS:
            self.esem[e] = self.sem("sm_" + e)
        self.waited = {e: {} for e in ENGS}
        self.nblk = 0
        self._uid = 0
        self.fence_sems = None

    def uid(self):
        self._uid += 1
        return self._uid

    def sem(self, name):
        s = Sem(self.nc.alloc_semaphore(name=name), name)
        self.all_sems.append(s)
        return s

    def op(self, eng, emit, deps=(), sem=None, amt=1):
        s = sem if sem is not None else self.esem[eng]
        s.n += amt
        tok = (s, s.n)
        self.ops.append((eng, tuple(deps), emit, s, amt))
        return tok

    def dma(self, queue, out, in_, sem, deps=()):
        return self.op(queue, lambda E, o=out, i=in_: E.dma_start(out=o, in_=i),
                       deps, sem, 16)

    def snap_compute(self):
        return [(self.esem[e], self.esem[e].n) for e in ("tensor", "scalar", "vector")
                if self.esem[e].n > 0]

    def fence(self, engines=("tensor", "scalar", "vector")):
        fs = self.fence_sems if self.fence_sems is not None else self.all_sems
        snap = tuple((s, s.n) for s in fs if s.n > 0)
        for e in engines:
            self.ops.append((e, "FENCE", snap, None, None))

    def flush(self, name=None):
        self.nblk += 1
        return

    def final_flush(self, name=None):
        if not self.ops:
            return
        ops, self.ops = self.ops, []
        self.nblk += 1
        by_eng = {e: [] for e in ENGS}
        for rec in ops:
            by_eng[rec[0]].append(rec)
        with self.nc.Block(name or f"b{self.nblk}", no_gpsimd_drain=True) as blk:
            for e in ENGS:
                if not by_eng[e]:
                    continue

                def body(E, eng=e, recs=by_eng[e]):
                    w = self.waited[eng]
                    for rec in recs:
                        if rec[1] == "FENCE":
                            for (ds, dv) in rec[2]:
                                if w.get(ds, 0) < dv:
                                    E.wait_ge(ds.h, dv)
                                    w[ds] = dv
                            continue
                        (_, deps, emit, s, amt) = rec
                        for (ds, dv) in deps:
                            if w.get(ds, 0) < dv:
                                E.wait_ge(ds.h, dv)
                                w[ds] = dv
                        emit(E).then_inc(s.h, amt)

                getattr(blk, e)(body)


def build_program(n_sub=N_SUB):
    nc = bass.Bass("TRN2", target_bir_lowering=False)
    pb = PB(nc)
    tab = _host_tables()
    gamC = tab["gamC"]

    dr = {}
    dr["x_in"] = nc.dram_tensor("x_in", [128, 8, T], F32, kind="ExternalInput").ap()
    y_out = nc.dram_tensor("y_out", [128, 8, T], F32, kind="ExternalOutput").ap()
    for nm, sh in (("aWq", [D, D]), ("aWk", [D, D]), ("aWv", [D, VDIM]),
                   ("aWg", [D, VDIM]), ("aWo", [VDIM, D]),
                   ("bWq", [D, D]), ("bWk", [D, D]), ("bWv", [D, D]), ("bWo", [D, D])):
        dr[nm] = nc.dram_tensor(nm, sh, F32, kind="ExternalInput").ap()
    dr["g_fm"] = nc.dram_tensor("g_fm", [128, 2], F32, kind="ExternalInput").ap()
    dr["bq_fm"] = nc.dram_tensor("bq_fm", [128, 8], F32, kind="ExternalInput").ap()
    dr["bk_fm"] = nc.dram_tensor("bk_fm", [128, 8], F32, kind="ExternalInput").ap()
    dr["bv_fm"] = nc.dram_tensor("bv_fm", [128, 8], F32, kind="ExternalInput").ap()
    dr["fgT"] = nc.dram_tensor("fgT", [D, FH], I32, kind="ExternalInput").ap()
    dr["fuT"] = nc.dram_tensor("fuT", [D, FH], I32, kind="ExternalInput").ap()
    dr["fdT"] = nc.dram_tensor("fdT", [FH, D], I32, kind="ExternalInput").ap()

    cd = {k: nc.inline_tensor(np.ascontiguousarray(v), name="c_" + k).ap()
          for k, v in tab.items() if isinstance(v, np.ndarray)}

    x_sb = nc.alloc_sbuf_tensor("x_fm", [128, 8, T], F32)
    xn = nc.alloc_sbuf_tensor("xn_fm", [128, 8, 1024], BF16)
    ident = nc.alloc_sbuf_tensor("sb_ident", [128, 128], BF16)
    inv1024 = nc.alloc_sbuf_tensor("sb_i1024", [128, 128], BF16)
    inv256 = nc.alloc_sbuf_tensor("sb_i256", [128, 128], BF16)
    kdec_sb = nc.alloc_sbuf_tensor("sb_kdec", [128, NH], F32)
    ones65 = nc.alloc_sbuf_tensor("sb_ones65", [65, 64], BF16)
    g_fm = nc.alloc_sbuf_tensor("sb_gfm", [128, 2], F32)
    bq_sb = nc.alloc_sbuf_tensor("sb_bq", [128, 8], F32)
    bk_sb = nc.alloc_sbuf_tensor("sb_bk", [128, 8], F32)
    bv_sb = nc.alloc_sbuf_tensor("sb_bv", [128, 8], F32)
    eps_sb = nc.alloc_sbuf_tensor("sb_eps", [128, 1], F32)

    sem_c = pb.sem("dm_c")
    sem_io = pb.sem("dm_io")
    sem_w = [pb.sem(f"dm_w{i}") for i in range(6)]

    pb.fence_sems = [pb.esem[e] for e in ENGS] + [sem_c, sem_io]
    pb.dma("sync", x_sb.ap()[:], dr["x_in"], sem_io)
    for dst, csrc in ((ident, cd["ident"]), (inv1024, cd["inv1024"]),
                     (inv256, cd["inv256"]),
                     (kdec_sb, cd["kdec"]), (ones65, cd["ones65"]),
                     (eps_sb, cd["epsc"]),
                     (g_fm, dr["g_fm"]), (bq_sb, dr["bq_fm"]),
                     (bk_sb, dr["bk_fm"]), (bv_sb, dr["bv_fm"])):
        pb.dma("sync", dst.ap()[:], csrc, sem_c)
    pb.fence()
    pb.flush("setup")

    wsem_res = {s: Res() for s in sem_w}
    _wi = [0]

    def wload(out_ap, in_ap, extra_deps=(), queue=None):
        s = sem_w[_wi[0] % len(sem_w)]
        if queue is None:
            queue = "sync" if _wi[0] % 2 == 0 else "scalar"
        _wi[0] += 1
        sres = wsem_res[s]
        deps = list(extra_deps) + sres.wdeps()
        tok = pb.dma(queue, out_ap, in_ap, s, deps=deps)
        sres.wrote(tok)
        return tok

    sem_st = [pb.sem(f"dm_st{i}") for i in range(2)]
    sem_pl = [pb.sem(f"dm_pl{i}") for i in range(2)]
    plres = [Res(), Res()]
    wscr = {}
    stage_end = {}
    stg = [nc.alloc_sbuf_tensor(f"stg{i}", [128, 2048], BF16) for i in range(2)]
    stgr = [Res(), Res()]
    _pi = [0]

    def prestage(name, K, F):
        scr = nc.dram_tensor(name + "_bfs", [128, F // 128, K, 128], BF16,
                             kind="Internal").ap()
        srcv = dr[name].rearrange("(k p) f -> p k f", p=128)
        for c in range(K):
            for (f0, f1) in ([(0, F)] if F <= 2048 else [(0, F // 2), (F // 2, F)]):
                w = f1 - f0
                i2 = _pi[0] % 2
                _pi[0] += 1
                t_l = pb.dma("gpsimd", stg[i2].ap()[:, 0:w], srcv[:, c, f0:f1],
                             sem_pl[i2], deps=stgr[i2].wdeps() + plres[i2].wdeps())
                plres[i2].wrote(t_l)
                stgr[i2].wrote(t_l)
                t_s = pb.dma("gpsimd", scr[:, f0 // 128:f1 // 128, c, :],
                             stg[i2].ap()[:, 0:w].rearrange("p (a b) -> p a b", b=128),
                             sem_st[i2], deps=[t_l])
                stgr[i2].read(t_s)
        wscr[name] = scr
        stage_end[name] = ((sem_st[0], sem_st[0].n), (sem_st[1], sem_st[1].n))

    for nm, K, F in (("aWq", 8, D), ("aWk", 8, D), ("aWv", 8, VDIM), ("aWg", 8, VDIM),
                     ("aWo", 16, D), ("fgT", 8, FH), ("fuT", 8, FH), ("fdT", 24, D),
                     ("bWq", 8, D), ("bWk", 8, D), ("bWv", 8, D), ("bWo", 8, D)):
        prestage(nm, K, F)
    pb.flush("prestage")

    def emit_ln(t_lo, t_len):
        pb.fence()
        with ExitStack() as st:
            bid = f"ln{pb.uid()}"
            xb = [st.enter_context(nc.sbuf_tensor(bid + f"xb{i}", [128, 512], BF16))
                  for i in range(4)]
            xq = [st.enter_context(nc.sbuf_tensor(bid + f"xq{i}", [128, 512], BF16))
                  for i in range(4)]
            xc = [st.enter_context(nc.sbuf_tensor(bid + f"xc{i}", [128, 512], F32))
                  for i in range(2)]
            rstd = [st.enter_context(nc.sbuf_tensor(bid + f"rs{i}", [128, 512], F32))
                    for i in range(2)]
            msq = rstd
            mps = [st.enter_context(nc.psum_tensor(bid + f"mp{i}", [128, 512], F32))
                   for i in range(2)]
            sps = [st.enter_context(nc.psum_tensor(bid + f"sp{i}", [128, 512], F32))
                   for i in range(2)]
            xbr = [Res() for _ in range(4)]
            xqr = [Res() for _ in range(4)]
            xcr = [Res(), Res()]
            msr = [Res(), Res()]
            rsr = [Res(), Res()]
            mpr = [Res(), Res()]
            spr = [Res(), Res()]
            for w in range(t_len // 512):
                w2 = w % 2
                lo = t_lo + w * 512
                xo = w * 512
                tm = ts = None
                for k in range(8):
                    i4 = (w * 8 + k) % 4
                    tb = pb.op("scalar", lambda E, o=xb[i4].ap()[:],
                               i=x_sb.ap()[:, k, lo:lo + 512]:
                               E.activation(o, i, AF.Copy), deps=xbr[i4].wdeps())
                    xbr[i4].wrote(tb)
                    tq = pb.op("scalar", lambda E, o=xq[i4].ap()[:],
                               i=x_sb.ap()[:, k, lo:lo + 512]:
                               E.activation(o, i, AF.Square), deps=xqr[i4].wdeps())
                    xqr[i4].wrote(tq)
                    tm = pb.op("tensor",
                               lambda E, o=mps[w2].ap()[:], l=inv1024.ap()[:],
                               r=xb[i4].ap()[:], st_=(k == 0), sp=(k == 7):
                               E.matmul(o, l, r, start=st_, stop=sp),
                               deps=[tb] + (mpr[w2].wdeps() if k == 0 else []))
                    xbr[i4].read(tm)
                    ts = pb.op("tensor",
                               lambda E, o=sps[w2].ap()[:], l=inv1024.ap()[:],
                               r=xq[i4].ap()[:], st_=(k == 0), sp=(k == 7):
                               E.matmul(o, l, r, start=st_, stop=sp),
                               deps=[tq] + (spr[w2].wdeps() if k == 0 else []))
                    xqr[i4].read(ts)
                mpr[w2].wrote(tm)
                spr[w2].wrote(ts)
                t1 = pb.op("scalar", lambda E, o=msq[w2].ap()[:], i=mps[w2].ap()[:]:
                           E.activation(o, i, AF.Square),
                           deps=[tm] + msr[w2].wdeps() + rsr[w2].wdeps())
                mpr[w2].read(t1)
                t2 = pb.op("vector", lambda E, o=msq[w2].ap()[:], a=sps[w2].ap()[:],
                           b=msq[w2].ap()[:]: E.tensor_tensor(o, a, b, ALU.subtract),
                           deps=[ts, t1])
                spr[w2].read(t2)
                t3 = pb.op("scalar", lambda E, o=rstd[w2].ap()[:], i=msq[w2].ap()[:],
                           EB=eps_sb.ap()[:, 0:1]:
                           E.activation(o, i, AF.Ln, bias=EB),
                           deps=[t2] + rsr[w2].wdeps())
                t4 = pb.op("scalar", lambda E, o=rstd[w2].ap()[:], i=rstd[w2].ap()[:]:
                           E.activation(o, i, AF.Exp, scale=-0.5), deps=[t3])
                rsr[w2].wrote(t4)
                msr[w2].wrote(t2)
                lastc = []
                for k in range(8):
                    i2 = k % 2
                    c1 = pb.op("vector",
                               lambda E, o=xc[i2].ap()[:],
                               a=x_sb.ap()[:, k, lo:lo + 512], b=mps[w2].ap()[:]:
                               E.tensor_tensor(o, a, b, ALU.subtract),
                               deps=[tm, t1] + xcr[i2].wdeps())
                    xcr[i2].wrote(c1)
                    mpr[w2].read(c1)
                    c2 = pb.op("vector",
                               lambda E, o=xn.ap()[:, k, xo:xo + 512],
                               a=xc[i2].ap()[:], b=rstd[w2].ap()[:]:
                               E.tensor_tensor(o, a, b, ALU.mult), deps=[c1, t4])
                    xcr[i2].read(c2)
                    lastc.append(c2)
                for c2 in lastc:
                    rsr[w2].read(c2)
            pb.fence()
            pb.flush(bid)

    def attn_a_quarter(hb, qo, S, Sbf, S_tok, S_fp, perm, lam_sb, gr_sb, slab, slr, asl_i):
        tb_abs = hb + qo
        with ExitStack() as st:
            bid = f"aq{pb.uid()}_"
            Qr = st.enter_context(nc.sbuf_tensor(bid + "Qr", [128, NH, 512], BF16))
            Kr = st.enter_context(nc.sbuf_tensor(bid + "Kr", [128, NH, 512], BF16))
            Kh = st.enter_context(nc.sbuf_tensor(bid + "Kh", [128, 4, NH, 128], BF16))

            if "qk" not in DEV_PHASES:
                return
            with ExitStack() as ps:
                rope = ps.enter_context(nc.sbuf_tensor(bid + "ro", [128, 4, 512], F32))
                rtok = []
                rope_seed = pb.snap_compute()
                for i, nm in enumerate(("ropeCq", "ropeSq", "ropeCk", "ropeSk")):
                    rtok.append(wload(rope.ap()[:, i, :], cd[nm][:, tb_abs:tb_abs + 512],
                                      extra_deps=rope_seed))
                qps = [ps.enter_context(nc.psum_tensor(bid + f"qp{i}", [128, 512], F32))
                       for i in range(4)]
                sws = [ps.enter_context(nc.psum_tensor(bid + f"sw{i}", [128, 512], F32))
                       for i in range(2)]
                ktp = [ps.enter_context(nc.psum_tensor(bid + f"kt{i}", [128, 128], BF16))
                       for i in range(2)]
                ktpr = [Res(), Res()]
                qpr = [Res() for _ in range(4)]
                swr = [Res(), Res()]
                qb = [ps.enter_context(nc.sbuf_tensor(bid + f"qb{i}", [128, 512], BF16))
                      for i in range(4)]
                qbr = [Res() for _ in range(4)]
                t12 = [ps.enter_context(nc.sbuf_tensor(bid + f"t12{i}", [128, 2, 512], F32))
                       for i in range(2)]
                t12r = [Res(), Res()]
                qk_i = [0]

                def qk_front(wnm, dst, h):
                    i2 = h % 2
                    i4 = qk_i[0] % 4
                    qk_i[0] += 1
                    si = asl_i[0] % 3
                    asl_i[0] += 1
                    sl, sres = slab[si], slr[si]
                    ltok = wload(
                        sl.ap()[:, 0:8, :], wscr[wnm][:, h, :, :],
                        extra_deps=sres.wdeps() + list(stage_end[wnm]))
                    sres.wrote(ltok)
                    qp, qres = qps[i4], qpr[i4]
                    tmm = None
                    mmdeps = qres.wdeps() + [ltok]
                    for k in range(8):
                        tmm = pb.op("tensor",
                                    lambda E, o=qp.ap()[:], l=sl.ap()[:, k, :],
                                    r=xn.ap()[:, k, qo:qo + 512],
                                    st_=(k == 0), sp=(k == 7):
                                    E.matmul(o, l, r, start=st_, stop=sp),
                                    deps=mmdeps if k == 0 else [])
                    qres.wrote(tmm)
                    sres.read(tmm)
                    tqb = pb.op("scalar", lambda E, o=qb[i4].ap()[:], i=qp.ap()[:]:
                                E.activation(o, i, AF.Copy),
                                deps=[tmm] + qbr[i4].wdeps())
                    qbr[i4].wrote(tqb)
                    qres.read(tqb)
                    return (tmm, tqb, i4)

                def qk_back(dst, ci, si, h, toks):
                    i2 = h % 2
                    tmm, tqb, i4 = toks
                    qp, qres = qps[i4], qpr[i4]
                    tsw = pb.op("tensor", lambda E, o=sws[i2].ap()[:],
                                l=perm.ap()[:], r=qb[i4].ap()[:]:
                                E.matmul(o, l, r, start=True, stop=True),
                                deps=[tqb] + swr[i2].wdeps())
                    swr[i2].wrote(tsw)
                    qbr[i4].read(tsw)
                    tt2 = pb.op("vector", lambda E, o=t12[i2].ap()[:, 1, :],
                                a=qp.ap()[:], b=rope.ap()[:, ci, :]:
                                E.tensor_tensor(o, a, b, ALU.mult),
                                deps=[tmm, tqb, rtok[ci]] + t12r[i2].wdeps())
                    qres.read(tt2)
                    tt1 = pb.op("vector", lambda E, o=t12[i2].ap()[:, 0, :],
                                a=sws[i2].ap()[:], b=rope.ap()[:, si, :]:
                                E.tensor_tensor(o, a, b, ALU.mult),
                                deps=[tsw, rtok[si]])
                    swr[i2].read(tt1)
                    tqr = pb.op("vector", lambda E, o=dst.ap()[:, h, :],
                                a=t12[i2].ap()[:, 0, :], b=t12[i2].ap()[:, 1, :]:
                                E.tensor_tensor(o, a, b, ALU.add),
                                deps=[tt1, tt2])
                    t12r[i2].wrote(tqr)
                    if dst is Kr:
                        for cl in range(4):
                            j2 = (h * 4 + cl) % 2
                            tkt = pb.op("tensor", lambda E, o=ktp[j2].ap()[:],
                                        i=Kr.ap()[:, h, cl * 128:(cl + 1) * 128],
                                        d=ident.ap()[:]: E.transpose(o, i, d),
                                        deps=[tqr] + ktpr[j2].wdeps())
                            ktpr[j2].wrote(tkt)
                            tkh = pb.op("scalar", lambda E,
                                        o=Kh.ap()[:, cl, h, :], i=ktp[j2].ap()[:],
                                        s=kdec_sb.ap()[:, h:h + 1]:
                                        E.activation(o, i, AF.Copy, scale=s),
                                        deps=[tkt])
                            ktpr[j2].read(tkh)

                qunits = [("aWq", Qr, 0, 1, h) for h in range(NH)] + \
                         [("aWk", Kr, 2, 3, h) for h in range(NH)]
                pend_qk = []
                for (wnm, dst, ci, si, h) in qunits:
                    toks = qk_front(wnm, dst, h)
                    pend_qk.append((dst, ci, si, h, toks))
                    if len(pend_qk) > 2:
                        qk_back(*pend_qk.pop(0))
                while pend_qk:
                    qk_back(*pend_qk.pop(0))
                pb.fence(("tensor",))
                pb.flush(bid + "qk")

            Vq = st.enter_context(nc.sbuf_tensor(bid + "Vq", [128, 4, VDIM], BF16))
            gs = st.enter_context(nc.sbuf_tensor(bid + "gs", [128, 16, 512], BF16))
            oq = st.enter_context(nc.sbuf_tensor(bid + "oq", [128, 16, 512], BF16))
            if "v" not in DEV_PHASES:
                return
            with ExitStack() as ps:
                vps = [ps.enter_context(nc.psum_tensor(bid + f"vp{i}", [128, 512], F32))
                       for i in range(2)]
                vtr = [ps.enter_context(nc.psum_tensor(bid + f"vt{i}", [128, 128], BF16))
                       for i in range(4)]
                gps = [ps.enter_context(nc.psum_tensor(bid + f"gp{i}", [128, 512], F32))
                       for i in range(2)]
                vpr = [Res(), Res()]
                vtrr = [Res() for _ in range(4)]
                gpr = [Res(), Res()]
                vfm = [ps.enter_context(nc.sbuf_tensor(bid + f"vf{i}", [128, 512], BF16))
                       for i in range(2)]
                vfr = [Res(), Res()]
                pend = []
                sli = [0]

                def mm_unit(kind, ec):
                    si = asl_i[0] % 3
                    asl_i[0] += 1
                    sl, sres = slab[si], slr[si]
                    wname = "aWv" if kind == "v" else "aWg"
                    ltok = wload(sl.ap()[:, 0:8, :], wscr[wname][:, ec, :, :],
                                 extra_deps=sres.wdeps() + list(stage_end[wname]))
                    sres.wrote(ltok)
                    pp, pres = (vps, vpr) if kind == "v" else (gps, gpr)
                    j2 = ec % 2
                    tmm = None
                    mmdeps = pres[j2].wdeps() + [ltok]
                    for k in range(8):
                        tmm = pb.op("tensor",
                                    lambda E, o=pp[j2].ap()[:], l=sl.ap()[:, k, :],
                                    r=xn.ap()[:, k, qo:qo + 512],
                                    st_=(k == 0), sp=(k == 7):
                                    E.matmul(o, l, r, start=st_, stop=sp),
                                    deps=mmdeps if k == 0 else [])
                    pres[j2].wrote(tmm)
                    sres.read(tmm)
                    if kind == "g":
                        tse = pb.op("scalar", lambda E, o=gs.ap()[:, ec, :],
                                    i=pp[j2].ap()[:]: E.activation(o, i, AF.Silu),
                                    deps=[tmm])
                        pres[j2].read(tse)
                        pb.op("vector", lambda E, o=gs.ap()[:, ec, :],
                              a=gs.ap()[:, ec, :],
                              s=g_fm.ap()[:, (ec % 2):(ec % 2) + 1]:
                              E.tensor_scalar(o, a, s, None, ALU.mult), deps=[tse])
                    else:
                        tev = pb.op("scalar", lambda E, o=vfm[j2].ap()[:],
                                    i=pp[j2].ap()[:]: E.activation(o, i, AF.Copy),
                                    deps=[tmm] + vfr[j2].wdeps())
                        vfr[j2].wrote(tev)
                        pres[j2].read(tev)
                        pend.append((ec, tev, j2))

                def v_trs():
                    ec, tev, i2 = pend.pop(0)
                    for tc in range(4):
                        j4 = (ec * 4 + tc) % 4
                        ttk = pb.op("tensor", lambda E, o=vtr[j4].ap()[:],
                                    i=vfm[i2].ap()[:, tc * 128:(tc + 1) * 128],
                                    d=ident.ap()[:]: E.transpose(o, i, d),
                                    deps=[tev] + vtrr[j4].wdeps())
                        vtrr[j4].wrote(ttk)
                        vfr[i2].read(ttk)
                        if tc % 2 == 0:
                            tcp = pb.op("vector",
                                        lambda E, o=Vq.ap()[:, tc, ec * 128:(ec + 1) * 128],
                                        i=vtr[j4].ap()[:]: E.tensor_copy(o, i),
                                        deps=[ttk])
                        else:
                            tcp = pb.op("scalar",
                                        lambda E, o=Vq.ap()[:, tc, ec * 128:(ec + 1) * 128],
                                        i=vtr[j4].ap()[:]: E.activation(o, i, AF.Copy),
                                        deps=[ttk])
                        vtrr[j4].read(tcp)

                for ec in range(16):
                    mm_unit("v", ec)
                    mm_unit("g", ec)
                    if pend and ec > 0:
                        v_trs()
                while pend:
                    v_trs()
                pb.fence(("tensor",))
                pb.flush(bid + "vg")

            if "mx" not in DEV_PHASES:
                return
            with ExitStack() as ps:
                scp = [ps.enter_context(nc.psum_tensor(bid + f"sc{i}", [128, 128], F32))
                       for i in range(2)]
                oap = [ps.enter_context(nc.psum_tensor(bid + f"oa{i}", [128, 128], F32))
                       for i in range(2)]
                obp = [ps.enter_context(nc.psum_tensor(bid + f"ob{i}", [128, 128], F32))
                       for i in range(2)]
                sup = [ps.enter_context(nc.psum_tensor(bid + f"su{i}", [128, 256], F32))
                       for i in range(2)]
                scr = [Res(), Res()]
                oar = [Res(), Res()]
                obr = [Res(), Res()]
                supr = [Res(), Res()]
                pT = [st.enter_context(nc.sbuf_tensor(bid + f"pT{i}", [128, 128], BF16))
                      for i in range(2)]
                pTr = [Res(), Res()]
                qt = [st.enter_context(nc.sbuf_tensor(bid + f"qt{i}", [128, 128], BF16))
                      for i in range(2)]
                qtr = [Res(), Res()]

                units = [(cl, h) for cl in range(4) for h in range(NH)]
                stash = {}

                def u_front(u):
                    cl, h = units[u]
                    cs = slice(cl * 128, (cl + 1) * 128)
                    i2 = u % 2
                    sc, scres = scp[i2], scr[i2]
                    t_sc = pb.op("tensor", lambda E, o=sc.ap()[:],
                                 l=Kr.ap()[:, h, cs], r=Qr.ap()[:, h, cs]:
                                 E.matmul(o, l, r, start=True, stop=True),
                                 deps=scres.wdeps())
                    scres.wrote(t_sc)
                    t_pT = pb.op("vector", lambda E, o=pT[i2].ap()[:],
                                 a=sc.ap()[:], b=lam_sb.ap()[:, h * 128:(h + 1) * 128]:
                                 E.tensor_tensor(o, a, b, ALU.mult),
                                 deps=[t_sc] + pTr[i2].wdeps())
                    pTr[i2].wrote(t_pT)
                    scres.read(t_pT)
                    t_qt = pb.op("vector", lambda E, o=qt[i2].ap()[:],
                                 a=Qr.ap()[:, h, cs],
                                 b=gr_sb.ap()[:, h * 128:(h + 1) * 128]:
                                 E.tensor_tensor(o, a, b, ALU.mult),
                                 deps=[t_sc] + qtr[i2].wdeps())
                    qtr[i2].wrote(t_qt)
                    stash[u] = (t_pT, t_qt)

                def u_back(u):
                    cl, h = units[u]
                    cs = slice(cl * 128, (cl + 1) * 128)
                    i2 = u % 2
                    t_pT, t_qt = stash.pop(u)
                    oa, oaR = oap[i2], oar[i2]
                    ob, obR = obp[i2], obr[i2]
                    su, suR = sup[i2], supr[i2]
                    t_o1 = pb.op("tensor", lambda E, o=oa.ap()[:],
                                 l=Vq.ap()[:, cl, h * 256:h * 256 + 128],
                                 r=pT[i2].ap()[:]:
                                 E.matmul(o, l, r, start=True, stop=False),
                                 deps=[t_pT] + oaR.wdeps())
                    t_o2 = pb.op("tensor", lambda E, o=ob.ap()[:],
                                 l=Vq.ap()[:, cl, h * 256 + 128:h * 256 + 256],
                                 r=pT[i2].ap()[:]:
                                 E.matmul(o, l, r, start=True, stop=False),
                                 deps=[t_pT] + obR.wdeps())
                    t_o3 = pb.op("tensor", lambda E, o=oa.ap()[:],
                                 l=Sbf.ap()[:, h, 0:128], r=qt[i2].ap()[:]:
                                 E.matmul(o, l, r, start=False, stop=True),
                                 deps=[t_qt, S_tok[h]])
                    t_o4 = pb.op("tensor", lambda E, o=ob.ap()[:],
                                 l=Sbf.ap()[:, h, 128:256], r=qt[i2].ap()[:]:
                                 E.matmul(o, l, r, start=False, stop=True),
                                 deps=[t_qt, S_tok[h]])
                    oaR.wrote(t_o3)
                    obR.wrote(t_o4)
                    pTr[i2].read(t_o2)
                    qtr[i2].read(t_o4)
                    t_sup = pb.op("tensor", lambda E, o=su.ap()[:],
                                  l=Kh.ap()[:, cl, h, :],
                                  r=Vq.ap()[:, cl, h * 256:(h + 1) * 256]:
                                  E.matmul(o, l, r, start=True, stop=True),
                                  deps=suR.wdeps())
                    suR.wrote(t_sup)
                    t_su = pb.op("vector", lambda E, o=S.ap()[:, h, :],
                                 a=S.ap()[:, h, :], g=gamC[h], b=su.ap()[:]:
                                 E.scalar_tensor_tensor(o, a, g, b, ALU.mult, ALU.add),
                                 deps=[t_sup, S_fp[h]])
                    S_fp[h] = t_su
                    suR.read(t_su)
                    t_sbf = pb.op("vector", lambda E, o=Sbf.ap()[:, h, :],
                                  i=S.ap()[:, h, :]: E.tensor_copy(o, i),
                                  deps=[t_su, t_o3, t_o4])
                    S_tok[h] = t_sbf
                    t_ea = pb.op("scalar", lambda E, o=oq.ap()[:, 2 * h, cs],
                                 i=oa.ap()[:]: E.activation(o, i, AF.Copy),
                                 deps=[t_o3])
                    t_eb = pb.op("vector", lambda E, o=oq.ap()[:, 2 * h + 1, cs],
                                 i=ob.ap()[:]: E.tensor_copy(o, i),
                                 deps=[t_o4])
                    oaR.read(t_ea)
                    obR.read(t_eb)

                for u in range(len(units)):
                    u_front(u)
                    if u > 0:
                        u_back(u - 1)
                u_back(len(units) - 1)
                pb.fence()
                pb.flush(bid + "mx1")

            with ExitStack() as ps:
                sqb = [ps.enter_context(nc.sbuf_tensor(bid + "sq", [128, 16, 128], BF16))] * 2
                sqr = [Res()] * 2
                rsb = [ps.enter_context(nc.sbuf_tensor(bid + "rs", [128, 16, 128], F32))] * 2
                rsr = [Res()] * 2
                mps = [ps.enter_context(nc.psum_tensor(bid + f"mp{i}", [128, 16, 128], F32))
                       for i in range(2)]
                mpr = [Res(), Res()]
                for cl in range(4):
                    cs = slice(cl * 128, (cl + 1) * 128)
                    i2 = cl % 2
                    t_sq = pb.op("scalar", lambda E, o=sqb[i2].ap()[:],
                                 i=oq.ap()[:, :, cs]: E.activation(o, i, AF.Square),
                                 deps=sqr[i2].wdeps())
                    sqr[i2].wrote(t_sq)
                    t_mm = None
                    mdeps = [t_sq] + mpr[i2].wdeps()
                    for h in range(NH):
                        for j in range(2):
                            t_mm = pb.op("tensor",
                                         lambda E, o=mps[i2].ap()[:, 2 * h + j, :],
                                         l=inv256.ap()[:], r=sqb[i2].ap()[:, 2 * h, :],
                                         st_=True, sp=False:
                                         E.matmul(o, l, r, start=True, stop=False),
                                         deps=mdeps if (h == 0 and j == 0) else [])
                            t_mm = pb.op("tensor",
                                         lambda E, o=mps[i2].ap()[:, 2 * h + j, :],
                                         l=inv256.ap()[:], r=sqb[i2].ap()[:, 2 * h + 1, :]:
                                         E.matmul(o, l, r, start=False, stop=True),
                                         deps=[])
                    mpr[i2].wrote(t_mm)
                    sqr[i2].read(t_mm)
                    t_ln = pb.op("scalar", lambda E, o=rsb[i2].ap()[:], i=mps[i2].ap()[:],
                                 EB=eps_sb.ap()[:, 0:1]:
                                 E.activation(o, i, AF.Ln, bias=EB),
                                 deps=[t_mm] + rsr[i2].wdeps())
                    mpr[i2].read(t_ln)
                    t_ex = pb.op("scalar", lambda E, o=rsb[i2].ap()[:], i=rsb[i2].ap()[:]:
                                 E.activation(o, i, AF.Exp, scale=-0.5), deps=[t_ln])
                    rsr[i2].wrote(t_ex)
                    t_n1 = pb.op("vector", lambda E, o=oq.ap()[:, :, cs],
                                 a=oq.ap()[:, :, cs], b=rsb[i2].ap()[:]:
                                 E.tensor_tensor(o, a, b, ALU.mult), deps=[t_ex])
                    rsr[i2].read(t_n1)
                    t_n2 = pb.op("vector", lambda E, o=oq.ap()[:, :, cs],
                                 a=oq.ap()[:, :, cs], b=gs.ap()[:, :, cs]:
                                 E.tensor_tensor(o, a, b, ALU.mult), deps=[t_n1])
                pb.fence(("tensor",))
                pb.flush(bid + "mx2")

            if "wo" not in DEV_PHASES:
                return
            with ExitStack() as ps:
                wop = [ps.enter_context(nc.psum_tensor(bid + f"wo{i}", [128, 512], F32))
                       for i in range(2)]
                wor = [Res(), Res()]
                for f in range(8):
                    i2 = f % 2
                    si = asl_i[0] % 3
                    asl_i[0] += 1
                    sl, sres = slab[si], slr[si]
                    ltok = wload(
                        sl.ap()[:], wscr["aWo"][:, f, :, :],
                        extra_deps=sres.wdeps() + list(stage_end["aWo"]))
                    sres.wrote(ltok)
                    wp, wres = wop[i2], wor[i2]
                    tmm = None
                    mmdeps = wres.wdeps() + [ltok]
                    for vk in range(16):
                        tmm = pb.op("tensor",
                                    lambda E, o=wp.ap()[:], l=sl.ap()[:, vk, :],
                                    r=oq.ap()[:, vk, :], st_=(vk == 0), sp=(vk == 15):
                                    E.matmul(o, l, r, start=st_, stop=sp),
                                    deps=mmdeps if vk == 0 else [])
                    wres.wrote(tmm)
                    sres.read(tmm)
                    tres = pb.op("vector",
                                 lambda E, o=x_sb.ap()[:, f, tb_abs:tb_abs + 512],
                                 a=wp.ap()[:], b=x_sb.ap()[:, f, tb_abs:tb_abs + 512]:
                                 E.scalar_tensor_tensor(o, a, 1.0, b, ALU.mult, ALU.add),
                                 deps=[tmm])
                    wres.read(tres)
                pb.fence(("tensor",))
                pb.flush(bid + "wo")

    def emit_attn_a():
        pb.fence()
        with ExitStack() as sub:
            S = sub.enter_context(nc.sbuf_tensor("ret_S", [128, NH, 256], F32))
            Sbf = sub.enter_context(nc.sbuf_tensor("ret_Sbf", [128, NH, 256], BF16))
            perm = sub.enter_context(nc.sbuf_tensor("sb_perm", [128, 128], BF16))
            lam_sb = sub.enter_context(nc.sbuf_tensor("sb_lam", [128, NH * 128], F32))
            gr_sb = sub.enter_context(nc.sbuf_tensor("sb_gr", [128, NH * 128], F32))
            aseed = pb.snap_compute()
            atok = []
            for dst_, csrc in ((perm, cd["perm"]), (lam_sb, cd["lamT"]),
                               (gr_sb, cd["gamR"])):
                atok.append(wload(dst_.ap()[:], csrc, extra_deps=aseed))
            slab = [sub.enter_context(nc.sbuf_tensor(f"asl{i}", [128, 16, 128], BF16))
                    for i in range(3)]
            slr = [Res(aseed) for _ in range(3)]
            asl_i = [0]
            t0 = pb.op("vector", lambda E: E.memset(S.ap()[:], 0.0))
            t1 = pb.op("vector", lambda E: E.memset(Sbf.ap()[:], 0.0), deps=[t0])
            S_tok = [t1] * NH
            S_fp = [t1] * NH
            pb.ops.append(("tensor", "FENCE", tuple(atok), None, None))
            pb.ops.append(("vector", "FENCE", tuple(atok), None, None))
            nq = 0
            for hb in (0, 1024):
                emit_ln(hb, 1024)
                for qo in (0, 512):
                    if nq < DEV_NQ:
                        attn_a_quarter(hb, qo, S, Sbf, S_tok, S_fp, perm, lam_sb, gr_sb, slab, slr, asl_i)
                        nq += 1
            pb.fence()
            pb.flush("enda")

    def emit_ffn():
        pb.fence()
        for hb in (0, 1024):
            emit_ln(hb, 1024)
            with ExitStack() as st:
                bid = f"ff{pb.uid()}_"
                H = st.enter_context(nc.sbuf_tensor(bid + "H", [128, 24, 1024], BF16))
                xng = st.enter_context(nc.sbuf_tensor(bid + "xg", [128, 8, 1024], BF16))
                t_xg = pb.op("vector", lambda E, o=xng.ap()[:], a=xn.ap()[:]:
                             E.tensor_scalar(o, a, SC_A, None, ALU.mult))
                slab = [st.enter_context(nc.sbuf_tensor(bid + f"sl{i}", [128, 24, 128], BF16))
                        for i in range(4)]
                slr = [Res(pb.snap_compute()) for _ in range(4)]
                with ExitStack() as ps:
                    gps = [ps.enter_context(nc.psum_tensor(bid + f"gp{i}", [128, 1024], F32))
                           for i in range(2)]
                    ups = [ps.enter_context(nc.psum_tensor(bid + f"up{i}", [128, 1024], F32))
                           for i in range(2)]
                    gpr = [Res(), Res()]
                    upr = [Res(), Res()]
                    for hf in range(24):
                        i2 = hf % 2
                        slg, slgr = slab[i2], slr[i2]
                        slu, slur = slab[2 + i2], slr[2 + i2]
                        ltg = wload(slg.ap()[:, 0:8, :], wscr["fgT"][:, hf, :, :],
                                    extra_deps=slgr.wdeps() + list(stage_end["fgT"]))
                        slgr.wrote(ltg)
                        ltu = wload(slu.ap()[:, 0:8, :], wscr["fuT"][:, hf, :, :],
                                    extra_deps=slur.wdeps() + list(stage_end["fuT"]))
                        slur.wrote(ltu)
                        gp, gres = gps[i2], gpr[i2]
                        up, ures = ups[i2], upr[i2]
                        tg = tu = None
                        gdeps = gres.wdeps() + [ltg]
                        udeps = ures.wdeps() + [ltu]
                        for k in range(8):
                            for tt in range(2):
                                tg = pb.op("tensor",
                                           lambda E, o=gp.ap()[:, tt * 512:(tt + 1) * 512],
                                           l=slg.ap()[:, k, :],
                                           r=xng.ap()[:, k, tt * 512:(tt + 1) * 512],
                                           st_=(k == 0), sp=(k == 7):
                                           E.matmul(o, l, r, start=st_, stop=sp),
                                           deps=(gdeps + [t_xg]) if (k == 0 and tt == 0) else [])
                        for k in range(8):
                            for tt in range(2):
                                tu = pb.op("tensor",
                                           lambda E, o=up.ap()[:, tt * 512:(tt + 1) * 512],
                                           l=slu.ap()[:, k, :],
                                           r=xn.ap()[:, k, tt * 512:(tt + 1) * 512],
                                           st_=(k == 0), sp=(k == 7):
                                           E.matmul(o, l, r, start=st_, stop=sp),
                                           deps=udeps if (k == 0 and tt == 0) else [])
                        gres.wrote(tg)
                        ures.wrote(tu)
                        slgr.read(tg)
                        slur.read(tu)
                        tsi = pb.op("scalar", lambda E, o=H.ap()[:, hf, :], i=gp.ap()[:]:
                                    E.activation(o, i, AF.Silu, scale=SC_B),
                                    deps=[tg])
                        gres.read(tsi)
                        th = pb.op("vector", lambda E, o=H.ap()[:, hf, :], a=up.ap()[:],
                                   b=H.ap()[:, hf, :]:
                                   E.scalar_tensor_tensor(o, a, 1.0 / 127.0, b,
                                                          ALU.mult, ALU.mult),
                                   deps=[tu, tsi])
                        ures.read(th)
                    pb.fence()
                    pb.flush(bid + "gu")
                with ExitStack() as ps:
                    dps = [ps.enter_context(nc.psum_tensor(bid + f"dp{i}", [128, 1024], F32))
                           for i in range(2)]
                    dpr = [Res(), Res()]
                    for f in range(8):
                        i2 = f % 2
                        sld, sldr = slab[i2], slr[i2]
                        ltd = wload(sld.ap()[:], wscr["fdT"][:, f, :, :],
                                    extra_deps=sldr.wdeps() + list(stage_end["fdT"]))
                        sldr.wrote(ltd)
                        dp, dres = dps[i2], dpr[i2]
                        td = None
                        ddeps = dres.wdeps() + [ltd]
                        for kf in range(24):
                            for tt in range(2):
                                td = pb.op("tensor",
                                           lambda E, o=dp.ap()[:, tt * 512:(tt + 1) * 512],
                                           l=sld.ap()[:, kf, :],
                                           r=H.ap()[:, kf, tt * 512:(tt + 1) * 512],
                                           st_=(kf == 0), sp=(kf == 23):
                                           E.matmul(o, l, r, start=st_, stop=sp),
                                           deps=ddeps if (kf == 0 and tt == 0) else [])
                        dres.wrote(td)
                        sldr.read(td)
                        tres = pb.op("vector",
                                     lambda E, o=x_sb.ap()[:, f, hb:hb + 1024],
                                     a=dp.ap()[:], b=x_sb.ap()[:, f, hb:hb + 1024]:
                                     E.scalar_tensor_tensor(o, a, 1.0 / 127.0, b,
                                                            ALU.mult, ALU.add),
                                     deps=[td])
                        dres.read(tres)
                    pb.fence()
                    pb.flush(bid + "dn")

    def emit_attn_b():
        pb.fence()
        with ExitStack() as sub:
            Kf = sub.enter_context(nc.sbuf_tensor("bK", [128, 8, T], BF16))
            Vt = sub.enter_context(nc.sbuf_tensor("bV", [128, 16, 16, 65], BF16))
            Qf = sub.enter_context(nc.sbuf_tensor("bQ", [128, 8, 1024], BF16))
            Of = sub.enter_context(nc.sbuf_tensor("bOf", [128, 8, 1024], BF16))
            slab = [sub.enter_context(nc.sbuf_tensor(f"bsl{i}", [128, 8, 128], BF16))
                    for i in range(2)]
            slr = [Res(pb.snap_compute()), Res(pb.snap_compute())]
            mA_sb = sub.enter_context(nc.sbuf_tensor("sb_mA", [128, 512], BF16))
            mB_sb = sub.enter_context(nc.sbuf_tensor("sb_mB", [128, 512], BF16))
            bseed = pb.snap_compute()
            btok = [wload(mA_sb.ap()[:], cd["maskA"], extra_deps=bseed),
                    wload(mB_sb.ap()[:], cd["maskB"], extra_deps=bseed)]
            pb.ops.append(("vector", "FENCE", tuple(btok), None, None))
            tones = pb.op("vector", lambda E: E.memset(Vt.ap()[:, :, :, 64:65], 1.0))

            def proj(wnm, bias_sb, dst, hb, dst_off, to_vt):
                with ExitStack() as ps:
                    bid = f"bp{pb.uid()}_"
                    pps = [ps.enter_context(nc.psum_tensor(bid + f"pp{i}", [128, 1024], F32))
                           for i in range(2)]
                    ppr = [Res(), Res()]
                    vtr = [ps.enter_context(nc.psum_tensor(bid + f"vt{i}", [128, 128], BF16))
                           for i in range(2)]
                    vtrr = [Res(), Res()]
                    vfm = [ps.enter_context(nc.sbuf_tensor(bid + f"vf{i}", [128, 1024], BF16))
                           for i in range(2)]
                    vfr = [Res(), Res()]
                    for f in range(8):
                        i2 = f % 2
                        sl, sres = slab[i2], slr[i2]
                        ltok = wload(
                            sl.ap()[:], wscr[wnm][:, f, :, :],
                            extra_deps=sres.wdeps() + list(stage_end[wnm]),
                            queue="sync")
                        sres.wrote(ltok)
                        pp, pres = pps[i2], ppr[i2]
                        tmm = None
                        mmdeps = pres.wdeps() + [ltok]
                        for k in range(8):
                            for tt in range(2):
                                tmm = pb.op("tensor",
                                            lambda E, o=pp.ap()[:, tt * 512:(tt + 1) * 512],
                                            l=sl.ap()[:, k, :],
                                            r=xn.ap()[:, k, tt * 512:(tt + 1) * 512],
                                            st_=(k == 0), sp=(k == 7):
                                            E.matmul(o, l, r, start=st_, stop=sp),
                                            deps=mmdeps if (k == 0 and tt == 0) else [])
                        pres.wrote(tmm)
                        sres.read(tmm)
                        if not to_vt:
                            tev = pb.op("scalar",
                                        lambda E, o=dst.ap()[:, f, dst_off:dst_off + 1024],
                                        i=pp.ap()[:], b=bias_sb.ap()[:, f:f + 1]:
                                        E.activation(o, i, AF.Identity, bias=b), deps=[tmm])
                            pres.read(tev)
                        else:
                            tev = pb.op("scalar", lambda E, o=vfm[i2].ap()[:],
                                        i=pp.ap()[:], b=bias_sb.ap()[:, f:f + 1]:
                                        E.activation(o, i, AF.Identity, bias=b),
                                        deps=[tmm] + vfr[i2].wdeps())
                            vfr[i2].wrote(tev)
                            pres.read(tev)
                            for tc in range(8):
                                j = hb // 128 + tc
                                j2 = tc % 2
                                ttk = pb.op("tensor", lambda E, o=vtr[j2].ap()[:],
                                            i=vfm[i2].ap()[:, tc * 128:(tc + 1) * 128],
                                            d=ident.ap()[:]: E.transpose(o, i, d),
                                            deps=[tev] + vtrr[j2].wdeps())
                                vtrr[j2].wrote(ttk)
                                vfr[i2].read(ttk)
                                tcp = pb.op("vector",
                                            lambda E,
                                            o=Vt.ap()[:, j, 2 * f:2 * f + 2, 0:64],
                                            i=vtr[j2].ap()[:].rearrange("p (a b) -> p a b", a=2):
                                            E.tensor_copy(o, i), deps=[ttk, tones])
                                vtrr[j2].read(tcp)
                    pb.fence(("tensor",))
                    pb.flush(bid)

            def mix(qh):
                with ExitStack() as ps:
                    bid = f"bm{qh}_{pb.uid()}_"
                    scp = [ps.enter_context(nc.psum_tensor(bid + f"sc{i}", [128, 1024], F32))
                           for i in range(2)]
                    scr = [Res(), Res()]
                    Ops = [ps.enter_context(nc.psum_tensor(bid + f"O{i}", [65, 1024], F32))
                           for i in range(2)]
                    Opr = [Res(), Res()]
                    expT = [ps.enter_context(nc.sbuf_tensor(bid + f"ex{i}", [128, 1024], BF16))
                            for i in range(2)]
                    expr = [Res(), Res()]
                    Oun = [ps.enter_context(nc.sbuf_tensor(bid + "ou", [65, 1024], BF16))] * 2
                    Our = [Res()] * 2
                    bcs = [ps.enter_context(nc.sbuf_tensor(bid + "bs", [64, 1024], F32))] * 2
                    bcsr = [Res()] * 2
                    dbf = [ps.enter_context(nc.sbuf_tensor(bid + "db", [1, 1024], BF16))] * 2
                    dbr = [Res()] * 2

                    steps = [(p, j, tt) for p in range(8) for j in range(16)
                             for tt in range(2)]
                    exp_tok = {}

                    def emit_scores(p, j, tt):
                        fch = p
                        i2 = (j * 2 + tt) % 2
                        sc, scres = scp[i2], scr[i2]
                        sdeps = scres.wdeps()
                        t_s = None
                        for hh in range(2):
                            t_s = pb.op("tensor",
                                        lambda E,
                                        o=sc.ap()[:, hh * 512:(hh + 1) * 512],
                                        l=Kf.ap()[hh * 64:hh * 64 + 64, fch, j * 128:(j + 1) * 128],
                                        r=Qf.ap()[hh * 64:hh * 64 + 64, fch, tt * 512:(tt + 1) * 512]:
                                        E.matmul(o, l, r, start=True, stop=True),
                                        deps=sdeps if hh == 0 else [])
                        scres.wrote(t_s)
                        t_e = pb.op("scalar", lambda E, o=expT[i2].ap()[:],
                                    i=sc.ap()[:]:
                                    E.activation(o, i, AF.Exp, scale=0.125),
                                    deps=[t_s] + expr[i2].wdeps())
                        expr[i2].wrote(t_e)
                        scres.read(t_e)
                        toks = [t_e, t_e]
                        if qh == 0 and j == 15 and tt == 0:
                            for hh in range(2):
                                tm = pb.op("vector",
                                           lambda E, o=expT[i2].ap()[:, hh * 512:(hh + 1) * 512],
                                           a=expT[i2].ap()[:, hh * 512:(hh + 1) * 512],
                                           b=mA_sb.ap()[:]:
                                           E.tensor_tensor(o, a, b, ALU.mult),
                                           deps=[t_e])
                                toks[hh] = tm
                                expr[i2].wrote(tm)
                        if qh == 1 and j == 0 and tt == 1:
                            for hh in range(2):
                                tm = pb.op("vector",
                                           lambda E, o=expT[i2].ap()[:, hh * 512:(hh + 1) * 512],
                                           a=expT[i2].ap()[:, hh * 512:(hh + 1) * 512],
                                           b=mB_sb.ap()[:]:
                                           E.tensor_tensor(o, a, b, ALU.mult),
                                           deps=[t_e])
                                toks[hh] = tm
                                expr[i2].wrote(tm)
                        exp_tok[(p, j, tt)] = toks

                    def emit_av(p, j, tt):
                        i2 = (j * 2 + tt) % 2
                        toks = exp_tok.pop((p, j, tt))
                        for hh in range(2):
                            h = 2 * p + hh
                            t_a = pb.op("tensor",
                                        lambda E, o=Ops[hh].ap()[:, tt * 512:(tt + 1) * 512],
                                        l=Vt.ap()[:, j, h, :],
                                        r=expT[i2].ap()[:, hh * 512:(hh + 1) * 512],
                                        st_=(j == 0), sp=(j == 15):
                                        E.matmul(o, l, r, start=st_, stop=sp),
                                        deps=[toks[hh]] + (Opr[hh].wdeps() if (j == 0 and tt == 0) else []))
                            expr[i2].read(t_a)
                            if j == 15 and tt == 1:
                                emit_epilogue(2 * p + hh, hh, t_a)

                    def emit_epilogue(h, hp, t_a):
                        pbase = (h % 2) * 64
                        fch = h // 2
                        evdeps = [t_a] + Our[hp].wdeps()
                        t_ev0 = pb.op("vector", lambda E, o=Oun[hp].ap()[0:64, :],
                                      i=Ops[hp].ap()[0:64, :]: E.tensor_copy(o, i),
                                      deps=evdeps)
                        t_ev = pb.op("vector", lambda E, o=Oun[hp].ap()[64:65, :],
                                     i=Ops[hp].ap()[64:65, :]: E.tensor_copy(o, i),
                                     deps=[t_a, t_ev0])
                        Our[hp].w = t_ev
                        Our[hp].r = [t_ev0]
                        t_bcv = None
                        for tt in range(2):
                            t_bc = pb.op("tensor",
                                         lambda E, o=Ops[hp].ap()[0:64, tt * 512:(tt + 1) * 512],
                                         l=ones65.ap()[64:65, :],
                                         r=Oun[hp].ap()[64:65, tt * 512:(tt + 1) * 512]:
                                         E.matmul(o, l, r, start=True, stop=True),
                                         deps=[t_ev, t_ev0])
                            Our[hp].read(t_bc)
                            t_bcv = pb.op("scalar",
                                          lambda E, o=bcs[hp].ap()[:, tt * 512:(tt + 1) * 512],
                                          i=Ops[hp].ap()[0:64, tt * 512:(tt + 1) * 512]:
                                          E.activation(o, i, AF.Ln),
                                          deps=[t_bc] + (bcsr[hp].wdeps() if tt == 0 else []))
                        bcsr[hp].wrote(t_bcv)
                        Opr[hp].w = t_bcv
                        Opr[hp].r = []
                        t_rc = pb.op("scalar", lambda E, o=bcs[hp].ap()[:],
                                     i=bcs[hp].ap()[:]:
                                     E.activation(o, i, AF.Exp, scale=-1.0),
                                     deps=[t_bcv])
                        bcsr[hp].wrote(t_rc)
                        t_on = pb.op("vector",
                                     lambda E, o=Of.ap()[pbase:pbase + 64, fch, :],
                                     a=Oun[hp].ap()[0:64, :], b=bcs[hp].ap()[:]:
                                     E.tensor_tensor(o, a, b, ALU.mult),
                                     deps=[t_rc])
                        Our[hp].read(t_on)
                        bcsr[hp].read(t_on)

                    for idx, stp in enumerate(steps):
                        emit_scores(*stp)
                        if idx > 0:
                            emit_av(*steps[idx - 1])
                    emit_av(*steps[-1])
                    pb.fence(("tensor",))
                    pb.flush(bid)

            def wo_half(qh):
                qoff = qh * 1024
                with ExitStack() as ps:
                    bid = f"bw{qh}_{pb.uid()}_"
                    wops = [ps.enter_context(nc.psum_tensor(bid + f"w{i}", [128, 512], F32))
                            for i in range(2)]
                    wopr = [Res(), Res()]
                    for f in range(8):
                        i2 = f % 2
                        sl, sres = slab[i2], slr[i2]
                        ltok = wload(
                            sl.ap()[:], wscr["bWo"][:, f, :, :],
                            extra_deps=sres.wdeps() + list(stage_end["bWo"]),
                            queue="sync")
                        sres.wrote(ltok)
                        for tt in range(2):
                            wp, wres = wops[(f * 2 + tt) % 2], wopr[(f * 2 + tt) % 2]
                            tmm = None
                            mmdeps = wres.wdeps() + [ltok]
                            for k in range(8):
                                tmm = pb.op("tensor",
                                            lambda E, o=wp.ap()[:],
                                            l=sl.ap()[:, k, :],
                                            r=Of.ap()[:, k, tt * 512:(tt + 1) * 512],
                                            st_=(k == 0), sp=(k == 7):
                                            E.matmul(o, l, r, start=st_, stop=sp),
                                            deps=mmdeps if k == 0 else [])
                            wres.wrote(tmm)
                            sres.read(tmm)
                            lo = qoff + tt * 512
                            tres = pb.op("vector",
                                         lambda E, o=x_sb.ap()[:, f, lo:lo + 512],
                                         a=wp.ap()[:], b=x_sb.ap()[:, f, lo:lo + 512]:
                                         E.scalar_tensor_tensor(o, a, 1.0, b,
                                                                ALU.mult, ALU.add),
                                         deps=[tmm])
                            wres.read(tres)
                    pb.fence(("tensor",))
                    pb.flush(bid)

            emit_ln(0, 1024)
            proj("bWk", bk_sb, Kf, 0, 0, False)
            proj("bWv", bv_sb, None, 0, 0, True)
            proj("bWq", bq_sb, Qf, 0, 0, False)
            emit_ln(1024, 1024)
            proj("bWk", bk_sb, Kf, 1024, 1024, False)
            proj("bWv", bv_sb, None, 1024, 0, True)
            mix(0)
            wo_half(0)
            proj("bWq", bq_sb, Qf, 1024, 0, False)
            mix(1)
            wo_half(1)
            pb.fence()
            pb.flush("endb")

    subs = [emit_attn_a, emit_ffn, emit_attn_b, emit_ffn]
    for i in range(n_sub):
        subs[i]()

    pb.fence()
    tst = pb.dma("sync", y_out, x_sb.ap()[:], sem_io, deps=pb.snap_compute())
    pb.op("sync", lambda E: E.nop(), deps=[tst])
    pb.final_flush("prog")
    return nc


_NC_CACHE = {}


def _get_nc(n_sub):
    if n_sub not in _NC_CACHE:
        _NC_CACHE[n_sub] = build_program(n_sub)
    return _NC_CACHE[n_sub]


def kernel(L0_attn_a_hidden_states, L0_attn_a_Wq, L0_attn_a_Wk, L0_attn_a_Wv,
           L0_attn_a_Wg, L0_attn_a_Wo, L0_attn_a_g_norm_w,
           L0_attn_b_Wq, L0_attn_b_bq, L0_attn_b_Wk, L0_attn_b_bk,
           L0_attn_b_Wv, L0_attn_b_bv, L0_attn_b_Wo,
           L0_ffn_w_gate, L0_ffn_w_up, L0_ffn_w_down):
    return _kernel_impl(L0_attn_a_hidden_states, L0_attn_a_Wq, L0_attn_a_Wk,
                        L0_attn_a_Wv, L0_attn_a_Wg, L0_attn_a_Wo, L0_attn_a_g_norm_w,
                        L0_attn_b_Wq, L0_attn_b_bq, L0_attn_b_Wk, L0_attn_b_bk,
                        L0_attn_b_Wv, L0_attn_b_bv, L0_attn_b_Wo,
                        L0_ffn_w_gate, L0_ffn_w_up, L0_ffn_w_down)


def host_in_maps(L0_attn_a_hidden_states, L0_attn_a_Wq, L0_attn_a_Wk, L0_attn_a_Wv,
                 L0_attn_a_Wg, L0_attn_a_Wo, L0_attn_a_g_norm_w,
                 L0_attn_b_Wq, L0_attn_b_bq, L0_attn_b_Wk, L0_attn_b_bk,
                 L0_attn_b_Wv, L0_attn_b_bv, L0_attn_b_Wo,
                 L0_ffn_w_gate, L0_ffn_w_up, L0_ffn_w_down):
    x = np.asarray(L0_attn_a_hidden_states, dtype=np.float32)
    B = x.shape[0]
    com = {
        "aWq": np.ascontiguousarray(np.asarray(L0_attn_a_Wq, np.float32)),
        "aWk": np.ascontiguousarray(np.asarray(L0_attn_a_Wk, np.float32)),
        "aWv": np.ascontiguousarray(np.asarray(L0_attn_a_Wv, np.float32)),
        "aWg": np.ascontiguousarray(np.asarray(L0_attn_a_Wg, np.float32)),
        "aWo": np.ascontiguousarray(np.asarray(L0_attn_a_Wo, np.float32)),
        "bWq": np.ascontiguousarray(np.asarray(L0_attn_b_Wq, np.float32).T),
        "bWk": np.ascontiguousarray(np.asarray(L0_attn_b_Wk, np.float32).T),
        "bWv": np.ascontiguousarray(np.asarray(L0_attn_b_Wv, np.float32).T),
        "bWo": np.ascontiguousarray(np.asarray(L0_attn_b_Wo, np.float32).T),
        "g_fm": np.ascontiguousarray(
            np.asarray(L0_attn_a_g_norm_w, np.float32).reshape(2, 128).T),
        "bq_fm": np.ascontiguousarray(np.asarray(L0_attn_b_bq, np.float32).reshape(8, 128).T),
        "bk_fm": np.ascontiguousarray(np.asarray(L0_attn_b_bk, np.float32).reshape(8, 128).T),
        "bv_fm": np.ascontiguousarray(np.asarray(L0_attn_b_bv, np.float32).reshape(8, 128).T),
        "fgT": np.ascontiguousarray(np.asarray(L0_ffn_w_gate, np.int32).T),
        "fuT": np.ascontiguousarray(np.asarray(L0_ffn_w_up, np.int32).T),
        "fdT": np.ascontiguousarray(np.asarray(L0_ffn_w_down, np.int32).T),
    }
    in_maps = []
    for b in range(B):
        m = dict(com)
        xT = np.ascontiguousarray(x[b].T)
        m["x_in"] = np.ascontiguousarray(xT.reshape(8, 128, T).transpose(1, 0, 2))
        in_maps.append(m)
    return in_maps


def _kernel_impl(*args):
    in_maps = host_in_maps(*args)
    B = len(in_maps)
    nc = _get_nc(N_SUB)
    res = run_bass_kernel_spmd(nc, in_maps, core_ids=list(range(B)))
    outs = []
    for r in res.results:
        y = r["y_out"]
        yd = y.transpose(1, 0, 2).reshape(D, T)
        outs.append(np.ascontiguousarray(yd.T))
    return np.stack(outs, axis=0)


# version 28 (final)
# speedup vs baseline: 1.1419x; 1.1419x over previous
from contextlib import ExitStack

import numpy as np
import ml_dtypes
import concourse.bass as bass
import concourse.mybir as mybir
from concourse.bass_utils import run_bass_kernel_spmd

F32 = mybir.dt.float32
BF16 = mybir.dt.bfloat16
I32 = mybir.dt.int32
AF = mybir.ActivationFunctionType
ALU = mybir.AluOpType
BF = ml_dtypes.bfloat16

T = 2048
D = 1024
NH = 8
DQK = 128
DV = 256
VDIM = 2048
HB = 16
FH = 3072
CK = 128
EPS = 1e-5
SC_A = 0.088728561997413635
SC_B = 0.088742740452289581
N_SUB = 4
DEV_PHASES = ("qk", "v", "g", "mx", "wo")
DEV_NQ = 4
DEV_QK_LVL = 6
DEV_TAP = False

ENGS = ("tensor", "scalar", "vector", "gpsimd", "sync")


def _ref_f32_consts():
    try:
        import jax
        import jax.numpy as jnp
        cpu = jax.local_devices(backend="cpu")[0]
        with jax.default_device(cpu):
            inv = 1.0 / 10000.0 ** (jnp.arange(0, DQK, 2, dtype=jnp.float32) / DQK)
            ang = jnp.arange(T, dtype=jnp.float32)[:, None] * inv[None, :]
            cos = np.asarray(jnp.cos(ang))
            sin = np.asarray(jnp.sin(ang))
            s32 = np.asarray(jnp.log2(1.0 - jnp.exp2(-5.0 - jnp.arange(NH, dtype=jnp.float32))))
            dd = (jnp.arange(CK, dtype=jnp.float32)[None, :]
                  - jnp.arange(CK, dtype=jnp.float32)[:, None])
            pair = np.stack([np.asarray(jnp.exp2(dd * jnp.float32(s32[h])))
                             for h in range(NH)], axis=0)
        return cos, sin, s32, pair
    except Exception:
        pass
    try:
        raise RuntimeError()
    except Exception:
        inv = (np.float32(1.0) /
               np.float32(10000.0) ** (np.arange(0, DQK, 2, dtype=np.float32) / np.float32(DQK)))
        ang = (np.arange(T, dtype=np.float32)[:, None] * inv[None, :]).astype(np.float32)
        s32f = np.log2(np.float32(1.0) - np.exp2(np.float32(-5.0)
                                                 - np.arange(NH, dtype=np.float32))).astype(np.float32)
        ddf = (np.arange(CK, dtype=np.float32)[None, :]
               - np.arange(CK, dtype=np.float32)[:, None]).astype(np.float32)
        pairf = np.stack([np.exp2((ddf * s32f[h]).astype(np.float32)).astype(np.float32)
                          for h in range(NH)], axis=0)
        return np.cos(ang).astype(np.float32), np.sin(ang).astype(np.float32), s32f, pairf


def _host_tables():
    t = {}
    cos, sin, s32, pair = _ref_f32_consts()
    cos64 = cos.astype(np.float64).T
    sin64 = sin.astype(np.float64).T
    C = np.zeros((DQK, T)); S = np.zeros((DQK, T))
    C[0::2, :] = cos64; C[1::2, :] = cos64
    S[0::2, :] = -sin64; S[1::2, :] = sin64
    qs = DQK ** (-0.5)
    t["ropeCq"] = (C * qs).astype(np.float32)
    t["ropeSq"] = (S * qs).astype(np.float32)
    t["ropeCk"] = C.astype(np.float32)
    t["ropeSk"] = S.astype(np.float32)
    s64 = s32.astype(np.float64)
    idx = np.arange(CK)
    lamT = np.zeros((CK, NH, CK), np.float32)
    gr = np.zeros((CK, NH, CK)); kd = np.zeros((CK, NH))
    dd = (idx[None, :] - idx[:, None]).astype(np.float32)
    for h in range(NH):
        lamT[:, h, :] = np.where(dd >= 0, pair[h], np.float32(0.0))
        gr[:, h, :] = 2.0 ** (s64[h] * idx[None, :])
        kd[:, h] = 2.0 ** (s64[h] * (CK - idx))
    t["lamT"] = lamT.reshape(CK, NH * CK).astype(np.float32)
    t["gamR"] = gr.reshape(CK, NH * CK).astype(np.float32)
    t["kdec"] = kd.astype(np.float32)
    t["gamC"] = [float(2.0 ** (s64[h] * CK)) for h in range(NH)]
    P = np.zeros((128, 128), np.float32)
    for m in range(128):
        P[m ^ 1, m] = 1.0
    t["perm"] = P.astype(BF)
    t["ident"] = np.eye(128, dtype=np.float32).astype(BF)
    t["inv1024"] = np.full((128, 128), 1.0 / 1024.0, np.float32).astype(BF)
    t["inv256"] = np.full((128, 128), 1.0 / 256.0, np.float32).astype(BF)
    t["ones65"] = np.ones((65, 64), np.float32).astype(BF)
    mA = np.ones((128, 512), np.float32); mA[127, 0] = 0.0
    mB = np.ones((128, 512), np.float32); mB[0, 511] = 0.0
    t["maskA"] = mA.astype(BF)
    t["maskB"] = mB.astype(BF)
    t["epsc"] = np.full((128, 1), EPS, np.float32)
    return t


class Sem:
    __slots__ = ("h", "n", "name")

    def __init__(self, h, name):
        self.h, self.n, self.name = h, 0, name


class Res:
    __slots__ = ("w", "r")

    def __init__(self, seed=()):
        self.w, self.r = None, list(seed)

    def wdeps(self):
        d = list(self.r)
        if self.w is not None:
            d.append(self.w)
        return d

    def wrote(self, tok):
        self.w, self.r = tok, []

    def read(self, tok):
        self.r.append(tok)


class PB:
    def __init__(self, nc):
        self.nc = nc
        self.ops = []
        self.all_sems = []
        self.esem = {}
        for e in ENGS:
            self.esem[e] = self.sem("sm_" + e)
        self.waited = {e: {} for e in ENGS}
        self.nblk = 0
        self._uid = 0
        self.fence_sems = None

    def uid(self):
        self._uid += 1
        return self._uid

    def sem(self, name):
        s = Sem(self.nc.alloc_semaphore(name=name), name)
        self.all_sems.append(s)
        return s

    def op(self, eng, emit, deps=(), sem=None, amt=1):
        s = sem if sem is not None else self.esem[eng]
        s.n += amt
        tok = (s, s.n)
        self.ops.append((eng, tuple(deps), emit, s, amt))
        return tok

    def dma(self, queue, out, in_, sem, deps=()):
        return self.op(queue, lambda E, o=out, i=in_: E.dma_start(out=o, in_=i),
                       deps, sem, 16)

    def snap_compute(self):
        return [(self.esem[e], self.esem[e].n) for e in ("tensor", "scalar", "vector")
                if self.esem[e].n > 0]

    def fence(self, engines=("tensor", "scalar", "vector")):
        fs = self.fence_sems if self.fence_sems is not None else self.all_sems
        snap = tuple((s, s.n) for s in fs if s.n > 0)
        for e in engines:
            self.ops.append((e, "FENCE", snap, None, None))

    def flush(self, name=None):
        self.nblk += 1
        return

    def final_flush(self, name=None):
        if not self.ops:
            return
        ops, self.ops = self.ops, []
        self.nblk += 1
        by_eng = {e: [] for e in ENGS}
        for rec in ops:
            by_eng[rec[0]].append(rec)
        with self.nc.Block(name or f"b{self.nblk}", no_gpsimd_drain=True) as blk:
            for e in ENGS:
                if not by_eng[e]:
                    continue

                def body(E, eng=e, recs=by_eng[e]):
                    w = self.waited[eng]
                    for rec in recs:
                        if rec[1] == "FENCE":
                            for (ds, dv) in rec[2]:
                                if w.get(ds, 0) < dv:
                                    E.wait_ge(ds.h, dv)
                                    w[ds] = dv
                            continue
                        (_, deps, emit, s, amt) = rec
                        for (ds, dv) in deps:
                            if w.get(ds, 0) < dv:
                                E.wait_ge(ds.h, dv)
                                w[ds] = dv
                        emit(E).then_inc(s.h, amt)

                getattr(blk, e)(body)


def build_program(n_sub=N_SUB):
    nc = bass.Bass("TRN2", target_bir_lowering=False)
    pb = PB(nc)
    tab = _host_tables()
    gamC = tab["gamC"]

    dr = {}
    dr["x_in"] = nc.dram_tensor("x_in", [128, 8, T], F32, kind="ExternalInput").ap()
    y_out = nc.dram_tensor("y_out", [128, 8, T], F32, kind="ExternalOutput").ap()
    for nm, sh in (("aWq", [D, D]), ("aWk", [D, D]), ("aWv", [D, VDIM]),
                   ("aWg", [D, VDIM]), ("aWo", [VDIM, D]),
                   ("bWq", [D, D]), ("bWk", [D, D]), ("bWv", [D, D]), ("bWo", [D, D])):
        dr[nm] = nc.dram_tensor(nm, sh, F32, kind="ExternalInput").ap()
    dr["g_fm"] = nc.dram_tensor("g_fm", [128, 2], F32, kind="ExternalInput").ap()
    dr["bq_fm"] = nc.dram_tensor("bq_fm", [128, 8], F32, kind="ExternalInput").ap()
    dr["bk_fm"] = nc.dram_tensor("bk_fm", [128, 8], F32, kind="ExternalInput").ap()
    dr["bv_fm"] = nc.dram_tensor("bv_fm", [128, 8], F32, kind="ExternalInput").ap()
    dr["fgT"] = nc.dram_tensor("fgT", [D, FH], I32, kind="ExternalInput").ap()
    dr["fuT"] = nc.dram_tensor("fuT", [D, FH], I32, kind="ExternalInput").ap()
    dr["fdT"] = nc.dram_tensor("fdT", [FH, D], I32, kind="ExternalInput").ap()

    cd = {k: nc.inline_tensor(np.ascontiguousarray(v), name="c_" + k).ap()
          for k, v in tab.items() if isinstance(v, np.ndarray)}

    x_sb = nc.alloc_sbuf_tensor("x_fm", [128, 8, T], F32)
    xn = nc.alloc_sbuf_tensor("xn_fm", [128, 8, 1024], BF16)
    ident = nc.alloc_sbuf_tensor("sb_ident", [128, 128], BF16)
    inv1024 = nc.alloc_sbuf_tensor("sb_i1024", [128, 128], BF16)
    inv256 = nc.alloc_sbuf_tensor("sb_i256", [128, 128], BF16)
    kdec_sb = nc.alloc_sbuf_tensor("sb_kdec", [128, NH], F32)
    ones65 = nc.alloc_sbuf_tensor("sb_ones65", [65, 64], BF16)
    g_fm = nc.alloc_sbuf_tensor("sb_gfm", [128, 2], F32)
    bq_sb = nc.alloc_sbuf_tensor("sb_bq", [128, 8], F32)
    bk_sb = nc.alloc_sbuf_tensor("sb_bk", [128, 8], F32)
    bv_sb = nc.alloc_sbuf_tensor("sb_bv", [128, 8], F32)
    eps_sb = nc.alloc_sbuf_tensor("sb_eps", [128, 1], F32)

    sem_c = pb.sem("dm_c")
    sem_io = pb.sem("dm_io")
    sem_w = [pb.sem(f"dm_w{i}") for i in range(6)]

    pb.fence_sems = [pb.esem[e] for e in ENGS] + [sem_c, sem_io]
    pb.dma("sync", x_sb.ap()[:], dr["x_in"], sem_io)
    for dst, csrc in ((ident, cd["ident"]), (inv1024, cd["inv1024"]),
                     (inv256, cd["inv256"]),
                     (kdec_sb, cd["kdec"]), (ones65, cd["ones65"]),
                     (eps_sb, cd["epsc"]),
                     (g_fm, dr["g_fm"]), (bq_sb, dr["bq_fm"]),
                     (bk_sb, dr["bk_fm"]), (bv_sb, dr["bv_fm"])):
        pb.dma("sync", dst.ap()[:], csrc, sem_c)
    pb.fence()
    pb.flush("setup")

    wsem_res = {s: Res() for s in sem_w}
    _wi = [0]

    def wload(out_ap, in_ap, extra_deps=(), queue=None):
        s = sem_w[_wi[0] % len(sem_w)]
        if queue is None:
            queue = "sync"
        _wi[0] += 1
        sres = wsem_res[s]
        deps = list(extra_deps) + sres.wdeps()
        tok = pb.dma(queue, out_ap, in_ap, s, deps=deps)
        sres.wrote(tok)
        return tok

    sem_st = [pb.sem(f"dm_st{i}") for i in range(2)]
    sem_pl = [pb.sem(f"dm_pl{i}") for i in range(2)]
    plres = [Res(), Res()]
    wscr = {}
    stage_end = {}
    stg = [nc.alloc_sbuf_tensor(f"stg{i}", [128, 2048], BF16) for i in range(2)]
    stgr = [Res(), Res()]
    _pi = [0]

    def prestage(name, K, F):
        scr = nc.dram_tensor(name + "_bfs", [128, F // 128, K, 128], BF16,
                             kind="Internal").ap()
        srcv = dr[name].rearrange("(k p) f -> p k f", p=128)
        for c in range(K):
            for (f0, f1) in ([(0, F)] if F <= 2048 else [(0, F // 2), (F // 2, F)]):
                w = f1 - f0
                i2 = _pi[0] % 2
                _pi[0] += 1
                t_l = pb.dma("gpsimd", stg[i2].ap()[:, 0:w], srcv[:, c, f0:f1],
                             sem_pl[i2], deps=stgr[i2].wdeps() + plres[i2].wdeps())
                plres[i2].wrote(t_l)
                stgr[i2].wrote(t_l)
                t_s = pb.dma("gpsimd", scr[:, f0 // 128:f1 // 128, c, :],
                             stg[i2].ap()[:, 0:w].rearrange("p (a b) -> p a b", b=128),
                             sem_st[i2], deps=[t_l])
                stgr[i2].read(t_s)
        wscr[name] = scr
        stage_end[name] = ((sem_st[0], sem_st[0].n), (sem_st[1], sem_st[1].n))

    for nm, K, F in (("aWq", 8, D), ("aWk", 8, D), ("aWv", 8, VDIM), ("aWg", 8, VDIM),
                     ("aWo", 16, D), ("fgT", 8, FH), ("fuT", 8, FH), ("fdT", 24, D),
                     ("bWq", 8, D), ("bWk", 8, D), ("bWv", 8, D), ("bWo", 8, D)):
        prestage(nm, K, F)
    pb.flush("prestage")

    def emit_ln(t_lo, t_len):
        pb.fence()
        with ExitStack() as st:
            bid = f"ln{pb.uid()}"
            xb = [st.enter_context(nc.sbuf_tensor(bid + f"xb{i}", [128, 512], BF16))
                  for i in range(4)]
            xq = [st.enter_context(nc.sbuf_tensor(bid + f"xq{i}", [128, 512], BF16))
                  for i in range(4)]
            xc = [st.enter_context(nc.sbuf_tensor(bid + f"xc{i}", [128, 512], F32))
                  for i in range(2)]
            rstd = [st.enter_context(nc.sbuf_tensor(bid + f"rs{i}", [128, 512], F32))
                    for i in range(2)]
            msq = rstd
            mps = [st.enter_context(nc.psum_tensor(bid + f"mp{i}", [128, 512], F32))
                   for i in range(2)]
            sps = [st.enter_context(nc.psum_tensor(bid + f"sp{i}", [128, 512], F32))
                   for i in range(2)]
            xbr = [Res() for _ in range(4)]
            xqr = [Res() for _ in range(4)]
            xcr = [Res(), Res()]
            msr = [Res(), Res()]
            rsr = [Res(), Res()]
            mpr = [Res(), Res()]
            spr = [Res(), Res()]
            for w in range(t_len // 512):
                w2 = w % 2
                lo = t_lo + w * 512
                xo = w * 512
                tm = ts = None
                for k in range(8):
                    i4 = (w * 8 + k) % 4
                    tb = pb.op("scalar", lambda E, o=xb[i4].ap()[:],
                               i=x_sb.ap()[:, k, lo:lo + 512]:
                               E.activation(o, i, AF.Copy), deps=xbr[i4].wdeps())
                    xbr[i4].wrote(tb)
                    tq = pb.op("scalar", lambda E, o=xq[i4].ap()[:],
                               i=x_sb.ap()[:, k, lo:lo + 512]:
                               E.activation(o, i, AF.Square), deps=xqr[i4].wdeps())
                    xqr[i4].wrote(tq)
                    tm = pb.op("tensor",
                               lambda E, o=mps[w2].ap()[:], l=inv1024.ap()[:],
                               r=xb[i4].ap()[:], st_=(k == 0), sp=(k == 7):
                               E.matmul(o, l, r, start=st_, stop=sp),
                               deps=[tb] + (mpr[w2].wdeps() if k == 0 else []))
                    xbr[i4].read(tm)
                    ts = pb.op("tensor",
                               lambda E, o=sps[w2].ap()[:], l=inv1024.ap()[:],
                               r=xq[i4].ap()[:], st_=(k == 0), sp=(k == 7):
                               E.matmul(o, l, r, start=st_, stop=sp),
                               deps=[tq] + (spr[w2].wdeps() if k == 0 else []))
                    xqr[i4].read(ts)
                mpr[w2].wrote(tm)
                spr[w2].wrote(ts)
                t1 = pb.op("scalar", lambda E, o=msq[w2].ap()[:], i=mps[w2].ap()[:]:
                           E.activation(o, i, AF.Square),
                           deps=[tm] + msr[w2].wdeps() + rsr[w2].wdeps())
                mpr[w2].read(t1)
                t2 = pb.op("vector", lambda E, o=msq[w2].ap()[:], a=sps[w2].ap()[:],
                           b=msq[w2].ap()[:]: E.tensor_tensor(o, a, b, ALU.subtract),
                           deps=[ts, t1])
                spr[w2].read(t2)
                t3 = pb.op("scalar", lambda E, o=rstd[w2].ap()[:], i=msq[w2].ap()[:],
                           EB=eps_sb.ap()[:, 0:1]:
                           E.activation(o, i, AF.Ln, bias=EB),
                           deps=[t2] + rsr[w2].wdeps())
                t4 = pb.op("scalar", lambda E, o=rstd[w2].ap()[:], i=rstd[w2].ap()[:]:
                           E.activation(o, i, AF.Exp, scale=-0.5), deps=[t3])
                rsr[w2].wrote(t4)
                msr[w2].wrote(t2)
                lastc = []
                for k in range(8):
                    i2 = k % 2
                    c1 = pb.op("vector",
                               lambda E, o=xc[i2].ap()[:],
                               a=x_sb.ap()[:, k, lo:lo + 512], b=mps[w2].ap()[:]:
                               E.tensor_tensor(o, a, b, ALU.subtract),
                               deps=[tm, t1] + xcr[i2].wdeps())
                    xcr[i2].wrote(c1)
                    mpr[w2].read(c1)
                    c2 = pb.op("vector",
                               lambda E, o=xn.ap()[:, k, xo:xo + 512],
                               a=xc[i2].ap()[:], b=rstd[w2].ap()[:]:
                               E.tensor_tensor(o, a, b, ALU.mult), deps=[c1, t4])
                    xcr[i2].read(c2)
                    lastc.append(c2)
                for c2 in lastc:
                    rsr[w2].read(c2)
            pb.fence()
            pb.flush(bid)

    def attn_a_quarter(hb, qo, S, Sbf, S_tok, S_fp, perm, lam_sb, gr_sb, slab, slr, asl_i):
        tb_abs = hb + qo
        with ExitStack() as st:
            bid = f"aq{pb.uid()}_"
            Qr = st.enter_context(nc.sbuf_tensor(bid + "Qr", [128, NH, 512], BF16))
            Kr = st.enter_context(nc.sbuf_tensor(bid + "Kr", [128, NH, 512], BF16))
            Kh = st.enter_context(nc.sbuf_tensor(bid + "Kh", [128, 4, NH, 128], BF16))

            if "qk" not in DEV_PHASES:
                return
            with ExitStack() as ps:
                rope = ps.enter_context(nc.sbuf_tensor(bid + "ro", [128, 4, 512], F32))
                rtok = []
                rope_seed = pb.snap_compute()
                for i, nm in enumerate(("ropeCq", "ropeSq", "ropeCk", "ropeSk")):
                    rtok.append(wload(rope.ap()[:, i, :], cd[nm][:, tb_abs:tb_abs + 512],
                                      extra_deps=rope_seed))
                qps = [ps.enter_context(nc.psum_tensor(bid + f"qp{i}", [128, 512], F32))
                       for i in range(4)]
                sws = [ps.enter_context(nc.psum_tensor(bid + f"sw{i}", [128, 512], F32))
                       for i in range(2)]
                ktp = [ps.enter_context(nc.psum_tensor(bid + f"kt{i}", [128, 128], BF16))
                       for i in range(2)]
                ktpr = [Res(), Res()]
                qpr = [Res() for _ in range(4)]
                swr = [Res(), Res()]
                qb = [ps.enter_context(nc.sbuf_tensor(bid + f"qb{i}", [128, 512], BF16))
                      for i in range(4)]
                qbr = [Res() for _ in range(4)]
                t12 = [ps.enter_context(nc.sbuf_tensor(bid + f"t12{i}", [128, 2, 512], F32))
                       for i in range(2)]
                t12r = [Res(), Res()]
                qk_i = [0]

                def qk_front(wnm, dst, h):
                    i2 = h % 2
                    i4 = qk_i[0] % 4
                    qk_i[0] += 1
                    si = asl_i[0] % 3
                    asl_i[0] += 1
                    sl, sres = slab[si], slr[si]
                    ltok = wload(
                        sl.ap()[:, 0:8, :], wscr[wnm][:, h, :, :],
                        extra_deps=sres.wdeps() + list(stage_end[wnm]))
                    sres.wrote(ltok)
                    qp, qres = qps[i4], qpr[i4]
                    tmm = None
                    mmdeps = qres.wdeps() + [ltok]
                    for k in range(8):
                        tmm = pb.op("tensor",
                                    lambda E, o=qp.ap()[:], l=sl.ap()[:, k, :],
                                    r=xn.ap()[:, k, qo:qo + 512],
                                    st_=(k == 0), sp=(k == 7):
                                    E.matmul(o, l, r, start=st_, stop=sp),
                                    deps=mmdeps if k == 0 else [])
                    qres.wrote(tmm)
                    sres.read(tmm)
                    tqb = pb.op("scalar", lambda E, o=qb[i4].ap()[:], i=qp.ap()[:]:
                                E.activation(o, i, AF.Copy),
                                deps=[tmm] + qbr[i4].wdeps())
                    qbr[i4].wrote(tqb)
                    qres.read(tqb)
                    return (tmm, tqb, i4)

                def qk_back(dst, ci, si, h, toks):
                    i2 = h % 2
                    tmm, tqb, i4 = toks
                    qp, qres = qps[i4], qpr[i4]
                    tsw = pb.op("tensor", lambda E, o=sws[i2].ap()[:],
                                l=perm.ap()[:], r=qb[i4].ap()[:]:
                                E.matmul(o, l, r, start=True, stop=True),
                                deps=[tqb] + swr[i2].wdeps())
                    swr[i2].wrote(tsw)
                    qbr[i4].read(tsw)
                    tt2 = pb.op("vector", lambda E, o=t12[i2].ap()[:, 1, :],
                                a=qp.ap()[:], b=rope.ap()[:, ci, :]:
                                E.tensor_tensor(o, a, b, ALU.mult),
                                deps=[tmm, tqb, rtok[ci]] + t12r[i2].wdeps())
                    qres.read(tt2)
                    tt1 = pb.op("vector", lambda E, o=t12[i2].ap()[:, 0, :],
                                a=sws[i2].ap()[:], b=rope.ap()[:, si, :]:
                                E.tensor_tensor(o, a, b, ALU.mult),
                                deps=[tsw, rtok[si]])
                    swr[i2].read(tt1)
                    tqr = pb.op("vector", lambda E, o=dst.ap()[:, h, :],
                                a=t12[i2].ap()[:, 0, :], b=t12[i2].ap()[:, 1, :]:
                                E.tensor_tensor(o, a, b, ALU.add),
                                deps=[tt1, tt2])
                    t12r[i2].wrote(tqr)
                    if dst is Kr:
                        for cl in range(4):
                            j2 = (h * 4 + cl) % 2
                            tkt = pb.op("tensor", lambda E, o=ktp[j2].ap()[:],
                                        i=Kr.ap()[:, h, cl * 128:(cl + 1) * 128],
                                        d=ident.ap()[:]: E.transpose(o, i, d),
                                        deps=[tqr] + ktpr[j2].wdeps())
                            ktpr[j2].wrote(tkt)
                            tkh = pb.op("scalar", lambda E,
                                        o=Kh.ap()[:, cl, h, :], i=ktp[j2].ap()[:],
                                        s=kdec_sb.ap()[:, h:h + 1]:
                                        E.activation(o, i, AF.Copy, scale=s),
                                        deps=[tkt])
                            ktpr[j2].read(tkh)

                qunits = [("aWq", Qr, 0, 1, h) for h in range(NH)] + \
                         [("aWk", Kr, 2, 3, h) for h in range(NH)]
                pend_qk = []
                for (wnm, dst, ci, si, h) in qunits:
                    toks = qk_front(wnm, dst, h)
                    pend_qk.append((dst, ci, si, h, toks))
                    if len(pend_qk) > 2:
                        qk_back(*pend_qk.pop(0))
                while pend_qk:
                    qk_back(*pend_qk.pop(0))
                pb.fence(("tensor",))
                pb.flush(bid + "qk")

            Vq = st.enter_context(nc.sbuf_tensor(bid + "Vq", [128, 4, VDIM], BF16))
            gs = st.enter_context(nc.sbuf_tensor(bid + "gs", [128, 16, 512], BF16))
            oq = st.enter_context(nc.sbuf_tensor(bid + "oq", [128, 16, 512], BF16))
            if "v" not in DEV_PHASES:
                return
            with ExitStack() as ps:
                vps = [ps.enter_context(nc.psum_tensor(bid + f"vp{i}", [128, 512], F32))
                       for i in range(2)]
                vtr = [ps.enter_context(nc.psum_tensor(bid + f"vt{i}", [128, 128], BF16))
                       for i in range(4)]
                gps = [ps.enter_context(nc.psum_tensor(bid + f"gp{i}", [128, 512], F32))
                       for i in range(2)]
                vpr = [Res(), Res()]
                vtrr = [Res() for _ in range(4)]
                gpr = [Res(), Res()]
                vfm = [ps.enter_context(nc.sbuf_tensor(bid + f"vf{i}", [128, 512], BF16))
                       for i in range(2)]
                vfr = [Res(), Res()]
                pend = []
                sli = [0]

                def mm_unit(kind, ec):
                    si = asl_i[0] % 3
                    asl_i[0] += 1
                    sl, sres = slab[si], slr[si]
                    wname = "aWv" if kind == "v" else "aWg"
                    ltok = wload(sl.ap()[:, 0:8, :], wscr[wname][:, ec, :, :],
                                 extra_deps=sres.wdeps() + list(stage_end[wname]))
                    sres.wrote(ltok)
                    pp, pres = (vps, vpr) if kind == "v" else (gps, gpr)
                    j2 = ec % 2
                    tmm = None
                    mmdeps = pres[j2].wdeps() + [ltok]
                    for k in range(8):
                        tmm = pb.op("tensor",
                                    lambda E, o=pp[j2].ap()[:], l=sl.ap()[:, k, :],
                                    r=xn.ap()[:, k, qo:qo + 512],
                                    st_=(k == 0), sp=(k == 7):
                                    E.matmul(o, l, r, start=st_, stop=sp),
                                    deps=mmdeps if k == 0 else [])
                    pres[j2].wrote(tmm)
                    sres.read(tmm)
                    if kind == "g":
                        tse = pb.op("scalar", lambda E, o=gs.ap()[:, ec, :],
                                    i=pp[j2].ap()[:]: E.activation(o, i, AF.Silu),
                                    deps=[tmm])
                        pres[j2].read(tse)
                        pb.op("vector", lambda E, o=gs.ap()[:, ec, :],
                              a=gs.ap()[:, ec, :],
                              s=g_fm.ap()[:, (ec % 2):(ec % 2) + 1]:
                              E.tensor_scalar(o, a, s, None, ALU.mult), deps=[tse])
                    else:
                        tev = pb.op("scalar", lambda E, o=vfm[j2].ap()[:],
                                    i=pp[j2].ap()[:]: E.activation(o, i, AF.Copy),
                                    deps=[tmm] + vfr[j2].wdeps())
                        vfr[j2].wrote(tev)
                        pres[j2].read(tev)
                        pend.append((ec, tev, j2))

                def v_trs():
                    ec, tev, i2 = pend.pop(0)
                    for tc in range(4):
                        j4 = (ec * 4 + tc) % 4
                        ttk = pb.op("tensor", lambda E, o=vtr[j4].ap()[:],
                                    i=vfm[i2].ap()[:, tc * 128:(tc + 1) * 128],
                                    d=ident.ap()[:]: E.transpose(o, i, d),
                                    deps=[tev] + vtrr[j4].wdeps())
                        vtrr[j4].wrote(ttk)
                        vfr[i2].read(ttk)
                        if tc % 2 == 0:
                            tcp = pb.op("vector",
                                        lambda E, o=Vq.ap()[:, tc, ec * 128:(ec + 1) * 128],
                                        i=vtr[j4].ap()[:]: E.tensor_copy(o, i),
                                        deps=[ttk])
                        else:
                            tcp = pb.op("scalar",
                                        lambda E, o=Vq.ap()[:, tc, ec * 128:(ec + 1) * 128],
                                        i=vtr[j4].ap()[:]: E.activation(o, i, AF.Copy),
                                        deps=[ttk])
                        vtrr[j4].read(tcp)

                for ec in range(16):
                    mm_unit("v", ec)
                    mm_unit("g", ec)
                    if pend and ec > 0:
                        v_trs()
                while pend:
                    v_trs()
                pb.fence(("tensor",))
                pb.flush(bid + "vg")

            if "mx" not in DEV_PHASES:
                return
            with ExitStack() as ps:
                scp = [ps.enter_context(nc.psum_tensor(bid + f"sc{i}", [128, 128], F32))
                       for i in range(2)]
                oap = [ps.enter_context(nc.psum_tensor(bid + f"oa{i}", [128, 128], F32))
                       for i in range(2)]
                obp = [ps.enter_context(nc.psum_tensor(bid + f"ob{i}", [128, 128], F32))
                       for i in range(2)]
                sup = [ps.enter_context(nc.psum_tensor(bid + f"su{i}", [128, 256], F32))
                       for i in range(2)]
                scr = [Res(), Res()]
                oar = [Res(), Res()]
                obr = [Res(), Res()]
                supr = [Res(), Res()]
                pT = [st.enter_context(nc.sbuf_tensor(bid + f"pT{i}", [128, 128], BF16))
                      for i in range(2)]
                pTr = [Res(), Res()]
                qt = [st.enter_context(nc.sbuf_tensor(bid + f"qt{i}", [128, 128], BF16))
                      for i in range(2)]
                qtr = [Res(), Res()]

                units = [(cl, h) for cl in range(4) for h in range(NH)]
                stash = {}

                def u_front(u):
                    cl, h = units[u]
                    cs = slice(cl * 128, (cl + 1) * 128)
                    i2 = u % 2
                    sc, scres = scp[i2], scr[i2]
                    t_sc = pb.op("tensor", lambda E, o=sc.ap()[:],
                                 l=Kr.ap()[:, h, cs], r=Qr.ap()[:, h, cs]:
                                 E.matmul(o, l, r, start=True, stop=True),
                                 deps=scres.wdeps())
                    scres.wrote(t_sc)
                    t_pT = pb.op("vector", lambda E, o=pT[i2].ap()[:],
                                 a=sc.ap()[:], b=lam_sb.ap()[:, h * 128:(h + 1) * 128]:
                                 E.tensor_tensor(o, a, b, ALU.mult),
                                 deps=[t_sc] + pTr[i2].wdeps())
                    pTr[i2].wrote(t_pT)
                    scres.read(t_pT)
                    t_qt = pb.op("vector", lambda E, o=qt[i2].ap()[:],
                                 a=Qr.ap()[:, h, cs],
                                 b=gr_sb.ap()[:, h * 128:(h + 1) * 128]:
                                 E.tensor_tensor(o, a, b, ALU.mult),
                                 deps=[t_sc] + qtr[i2].wdeps())
                    qtr[i2].wrote(t_qt)
                    stash[u] = (t_pT, t_qt)

                def u_back(u):
                    cl, h = units[u]
                    cs = slice(cl * 128, (cl + 1) * 128)
                    i2 = u % 2
                    t_pT, t_qt = stash.pop(u)
                    oa, oaR = oap[i2], oar[i2]
                    ob, obR = obp[i2], obr[i2]
                    su, suR = sup[i2], supr[i2]
                    t_o1 = pb.op("tensor", lambda E, o=oa.ap()[:],
                                 l=Vq.ap()[:, cl, h * 256:h * 256 + 128],
                                 r=pT[i2].ap()[:]:
                                 E.matmul(o, l, r, start=True, stop=False),
                                 deps=[t_pT] + oaR.wdeps())
                    t_o2 = pb.op("tensor", lambda E, o=ob.ap()[:],
                                 l=Vq.ap()[:, cl, h * 256 + 128:h * 256 + 256],
                                 r=pT[i2].ap()[:]:
                                 E.matmul(o, l, r, start=True, stop=False),
                                 deps=[t_pT] + obR.wdeps())
                    t_o3 = pb.op("tensor", lambda E, o=oa.ap()[:],
                                 l=Sbf.ap()[:, h, 0:128], r=qt[i2].ap()[:]:
                                 E.matmul(o, l, r, start=False, stop=True),
                                 deps=[t_qt, S_tok[h]])
                    t_o4 = pb.op("tensor", lambda E, o=ob.ap()[:],
                                 l=Sbf.ap()[:, h, 128:256], r=qt[i2].ap()[:]:
                                 E.matmul(o, l, r, start=False, stop=True),
                                 deps=[t_qt, S_tok[h]])
                    oaR.wrote(t_o3)
                    obR.wrote(t_o4)
                    pTr[i2].read(t_o2)
                    qtr[i2].read(t_o4)
                    t_sup = pb.op("tensor", lambda E, o=su.ap()[:],
                                  l=Kh.ap()[:, cl, h, :],
                                  r=Vq.ap()[:, cl, h * 256:(h + 1) * 256]:
                                  E.matmul(o, l, r, start=True, stop=True),
                                  deps=suR.wdeps())
                    suR.wrote(t_sup)
                    t_su = pb.op("vector", lambda E, o=S.ap()[:, h, :],
                                 a=S.ap()[:, h, :], g=gamC[h], b=su.ap()[:]:
                                 E.scalar_tensor_tensor(o, a, g, b, ALU.mult, ALU.add),
                                 deps=[t_sup, S_fp[h]])
                    S_fp[h] = t_su
                    suR.read(t_su)
                    t_sbf = pb.op("vector", lambda E, o=Sbf.ap()[:, h, :],
                                  i=S.ap()[:, h, :]: E.tensor_copy(o, i),
                                  deps=[t_su, t_o3, t_o4])
                    S_tok[h] = t_sbf
                    t_ea = pb.op("scalar", lambda E, o=oq.ap()[:, 2 * h, cs],
                                 i=oa.ap()[:]: E.activation(o, i, AF.Copy),
                                 deps=[t_o3])
                    t_eb = pb.op("vector", lambda E, o=oq.ap()[:, 2 * h + 1, cs],
                                 i=ob.ap()[:]: E.tensor_copy(o, i),
                                 deps=[t_o4])
                    oaR.read(t_ea)
                    obR.read(t_eb)

                for u in range(len(units)):
                    u_front(u)
                    if u > 0:
                        u_back(u - 1)
                u_back(len(units) - 1)
                pb.fence()
                pb.flush(bid + "mx1")

            with ExitStack() as ps:
                sqb = [ps.enter_context(nc.sbuf_tensor(bid + "sq", [128, 16, 128], BF16))] * 2
                sqr = [Res()] * 2
                rsb = [ps.enter_context(nc.sbuf_tensor(bid + "rs", [128, 16, 128], F32))] * 2
                rsr = [Res()] * 2
                mps = [ps.enter_context(nc.psum_tensor(bid + f"mp{i}", [128, 16, 128], F32))
                       for i in range(2)]
                mpr = [Res(), Res()]
                for cl in range(4):
                    cs = slice(cl * 128, (cl + 1) * 128)
                    i2 = cl % 2
                    t_sq = pb.op("scalar", lambda E, o=sqb[i2].ap()[:],
                                 i=oq.ap()[:, :, cs]: E.activation(o, i, AF.Square),
                                 deps=sqr[i2].wdeps())
                    sqr[i2].wrote(t_sq)
                    t_mm = None
                    mdeps = [t_sq] + mpr[i2].wdeps()
                    for h in range(NH):
                        for j in range(2):
                            t_mm = pb.op("tensor",
                                         lambda E, o=mps[i2].ap()[:, 2 * h + j, :],
                                         l=inv256.ap()[:], r=sqb[i2].ap()[:, 2 * h, :],
                                         st_=True, sp=False:
                                         E.matmul(o, l, r, start=True, stop=False),
                                         deps=mdeps if (h == 0 and j == 0) else [])
                            t_mm = pb.op("tensor",
                                         lambda E, o=mps[i2].ap()[:, 2 * h + j, :],
                                         l=inv256.ap()[:], r=sqb[i2].ap()[:, 2 * h + 1, :]:
                                         E.matmul(o, l, r, start=False, stop=True),
                                         deps=[])
                    mpr[i2].wrote(t_mm)
                    sqr[i2].read(t_mm)
                    t_ln = pb.op("scalar", lambda E, o=rsb[i2].ap()[:], i=mps[i2].ap()[:],
                                 EB=eps_sb.ap()[:, 0:1]:
                                 E.activation(o, i, AF.Ln, bias=EB),
                                 deps=[t_mm] + rsr[i2].wdeps())
                    mpr[i2].read(t_ln)
                    t_ex = pb.op("scalar", lambda E, o=rsb[i2].ap()[:], i=rsb[i2].ap()[:]:
                                 E.activation(o, i, AF.Exp, scale=-0.5), deps=[t_ln])
                    rsr[i2].wrote(t_ex)
                    t_n1 = pb.op("vector", lambda E, o=oq.ap()[:, :, cs],
                                 a=oq.ap()[:, :, cs], b=rsb[i2].ap()[:]:
                                 E.tensor_tensor(o, a, b, ALU.mult), deps=[t_ex])
                    rsr[i2].read(t_n1)
                    t_n2 = pb.op("vector", lambda E, o=oq.ap()[:, :, cs],
                                 a=oq.ap()[:, :, cs], b=gs.ap()[:, :, cs]:
                                 E.tensor_tensor(o, a, b, ALU.mult), deps=[t_n1])
                pb.fence(("tensor",))
                pb.flush(bid + "mx2")

            if "wo" not in DEV_PHASES:
                return
            with ExitStack() as ps:
                wop = [ps.enter_context(nc.psum_tensor(bid + f"wo{i}", [128, 512], F32))
                       for i in range(2)]
                wor = [Res(), Res()]
                for f in range(8):
                    i2 = f % 2
                    si = asl_i[0] % 3
                    asl_i[0] += 1
                    sl, sres = slab[si], slr[si]
                    ltok = wload(
                        sl.ap()[:], wscr["aWo"][:, f, :, :],
                        extra_deps=sres.wdeps() + list(stage_end["aWo"]))
                    sres.wrote(ltok)
                    wp, wres = wop[i2], wor[i2]
                    tmm = None
                    mmdeps = wres.wdeps() + [ltok]
                    for vk in range(16):
                        tmm = pb.op("tensor",
                                    lambda E, o=wp.ap()[:], l=sl.ap()[:, vk, :],
                                    r=oq.ap()[:, vk, :], st_=(vk == 0), sp=(vk == 15):
                                    E.matmul(o, l, r, start=st_, stop=sp),
                                    deps=mmdeps if vk == 0 else [])
                    wres.wrote(tmm)
                    sres.read(tmm)
                    tres = pb.op("vector",
                                 lambda E, o=x_sb.ap()[:, f, tb_abs:tb_abs + 512],
                                 a=wp.ap()[:], b=x_sb.ap()[:, f, tb_abs:tb_abs + 512]:
                                 E.scalar_tensor_tensor(o, a, 1.0, b, ALU.mult, ALU.add),
                                 deps=[tmm])
                    wres.read(tres)
                pb.fence(("tensor",))
                pb.flush(bid + "wo")

    def emit_attn_a():
        pb.fence()
        with ExitStack() as sub:
            S = sub.enter_context(nc.sbuf_tensor("ret_S", [128, NH, 256], F32))
            Sbf = sub.enter_context(nc.sbuf_tensor("ret_Sbf", [128, NH, 256], BF16))
            perm = sub.enter_context(nc.sbuf_tensor("sb_perm", [128, 128], BF16))
            lam_sb = sub.enter_context(nc.sbuf_tensor("sb_lam", [128, NH * 128], F32))
            gr_sb = sub.enter_context(nc.sbuf_tensor("sb_gr", [128, NH * 128], F32))
            aseed = pb.snap_compute()
            atok = []
            for dst_, csrc in ((perm, cd["perm"]), (lam_sb, cd["lamT"]),
                               (gr_sb, cd["gamR"])):
                atok.append(wload(dst_.ap()[:], csrc, extra_deps=aseed))
            slab = [sub.enter_context(nc.sbuf_tensor(f"asl{i}", [128, 16, 128], BF16))
                    for i in range(3)]
            slr = [Res(aseed) for _ in range(3)]
            asl_i = [0]
            t0 = pb.op("vector", lambda E: E.memset(S.ap()[:], 0.0))
            t1 = pb.op("vector", lambda E: E.memset(Sbf.ap()[:], 0.0), deps=[t0])
            S_tok = [t1] * NH
            S_fp = [t1] * NH
            pb.ops.append(("tensor", "FENCE", tuple(atok), None, None))
            pb.ops.append(("vector", "FENCE", tuple(atok), None, None))
            nq = 0
            for hb in (0, 1024):
                emit_ln(hb, 1024)
                for qo in (0, 512):
                    if nq < DEV_NQ:
                        attn_a_quarter(hb, qo, S, Sbf, S_tok, S_fp, perm, lam_sb, gr_sb, slab, slr, asl_i)
                        nq += 1
            pb.fence()
            pb.flush("enda")

    dn_last = {}

    def emit_ffn():
        pb.fence()
        for hb in (0, 1024):
            emit_ln(hb, 1024)
            with ExitStack() as st:
                bid = f"ff{pb.uid()}_"
                H = st.enter_context(nc.sbuf_tensor(bid + "H", [128, 24, 1024], BF16))
                xng = st.enter_context(nc.sbuf_tensor(bid + "xg", [128, 8, 1024], BF16))
                t_xg = pb.op("vector", lambda E, o=xng.ap()[:], a=xn.ap()[:]:
                             E.tensor_scalar(o, a, SC_A, None, ALU.mult))
                slab = [st.enter_context(nc.sbuf_tensor(bid + f"sl{i}", [128, 24, 128], BF16))
                        for i in range(4)]
                slr = [Res(pb.snap_compute()) for _ in range(4)]
                with ExitStack() as ps:
                    gps = [ps.enter_context(nc.psum_tensor(bid + f"gp{i}", [128, 1024], F32))
                           for i in range(2)]
                    ups = [ps.enter_context(nc.psum_tensor(bid + f"up{i}", [128, 1024], F32))
                           for i in range(2)]
                    gpr = [Res(), Res()]
                    upr = [Res(), Res()]
                    for hf in range(24):
                        i2 = hf % 2
                        slg, slgr = slab[i2], slr[i2]
                        slu, slur = slab[2 + i2], slr[2 + i2]
                        ltg = wload(slg.ap()[:, 0:8, :], wscr["fgT"][:, hf, :, :],
                                    extra_deps=slgr.wdeps() + list(stage_end["fgT"]))
                        slgr.wrote(ltg)
                        ltu = wload(slu.ap()[:, 0:8, :], wscr["fuT"][:, hf, :, :],
                                    extra_deps=slur.wdeps() + list(stage_end["fuT"]))
                        slur.wrote(ltu)
                        gp, gres = gps[i2], gpr[i2]
                        up, ures = ups[i2], upr[i2]
                        tg = tu = None
                        gdeps = gres.wdeps() + [ltg]
                        udeps = ures.wdeps() + [ltu]
                        for k in range(8):
                            for tt in range(2):
                                tg = pb.op("tensor",
                                           lambda E, o=gp.ap()[:, tt * 512:(tt + 1) * 512],
                                           l=slg.ap()[:, k, :],
                                           r=xng.ap()[:, k, tt * 512:(tt + 1) * 512],
                                           st_=(k == 0), sp=(k == 7):
                                           E.matmul(o, l, r, start=st_, stop=sp),
                                           deps=(gdeps + [t_xg]) if (k == 0 and tt == 0) else [])
                        for k in range(8):
                            for tt in range(2):
                                tu = pb.op("tensor",
                                           lambda E, o=up.ap()[:, tt * 512:(tt + 1) * 512],
                                           l=slu.ap()[:, k, :],
                                           r=xn.ap()[:, k, tt * 512:(tt + 1) * 512],
                                           st_=(k == 0), sp=(k == 7):
                                           E.matmul(o, l, r, start=st_, stop=sp),
                                           deps=udeps if (k == 0 and tt == 0) else [])
                        gres.wrote(tg)
                        ures.wrote(tu)
                        slgr.read(tg)
                        slur.read(tu)
                        tsi = pb.op("scalar", lambda E, o=H.ap()[:, hf, :], i=gp.ap()[:]:
                                    E.activation(o, i, AF.Silu, scale=SC_B),
                                    deps=[tg])
                        gres.read(tsi)
                        th = pb.op("vector", lambda E, o=H.ap()[:, hf, :], a=up.ap()[:],
                                   b=H.ap()[:, hf, :]:
                                   E.scalar_tensor_tensor(o, a, 1.0 / 127.0, b,
                                                          ALU.mult, ALU.mult),
                                   deps=[tu, tsi])
                        ures.read(th)
                    pb.fence()
                    pb.flush(bid + "gu")
                with ExitStack() as ps:
                    dps = [ps.enter_context(nc.psum_tensor(bid + f"dp{i}", [128, 1024], F32))
                           for i in range(2)]
                    dpr = [Res(), Res()]
                    for f in range(8):
                        i2 = f % 2
                        sld, sldr = slab[i2], slr[i2]
                        ltd = wload(sld.ap()[:], wscr["fdT"][:, f, :, :],
                                    extra_deps=sldr.wdeps() + list(stage_end["fdT"]))
                        sldr.wrote(ltd)
                        dp, dres = dps[i2], dpr[i2]
                        td = None
                        ddeps = dres.wdeps() + [ltd]
                        for kf in range(24):
                            for tt in range(2):
                                td = pb.op("tensor",
                                           lambda E, o=dp.ap()[:, tt * 512:(tt + 1) * 512],
                                           l=sld.ap()[:, kf, :],
                                           r=H.ap()[:, kf, tt * 512:(tt + 1) * 512],
                                           st_=(kf == 0), sp=(kf == 23):
                                           E.matmul(o, l, r, start=st_, stop=sp),
                                           deps=ddeps if (kf == 0 and tt == 0) else [])
                        dres.wrote(td)
                        sldr.read(td)
                        tres = pb.op("vector",
                                     lambda E, o=x_sb.ap()[:, f, hb:hb + 1024],
                                     a=dp.ap()[:], b=x_sb.ap()[:, f, hb:hb + 1024]:
                                     E.scalar_tensor_tensor(o, a, 1.0 / 127.0, b,
                                                            ALU.mult, ALU.add),
                                     deps=[td])
                        dres.read(tres)
                        dn_last[(hb, f)] = tres
                    pb.fence()
                    pb.flush(bid + "dn")

    def emit_attn_b():
        pb.fence()
        with ExitStack() as sub:
            Kf = sub.enter_context(nc.sbuf_tensor("bK", [128, 8, T], BF16))
            Vt = sub.enter_context(nc.sbuf_tensor("bV", [128, 16, 16, 65], BF16))
            Qf = sub.enter_context(nc.sbuf_tensor("bQ", [128, 8, 1024], BF16))
            Of = sub.enter_context(nc.sbuf_tensor("bOf", [128, 8, 1024], BF16))
            slab = [sub.enter_context(nc.sbuf_tensor(f"bsl{i}", [128, 8, 128], BF16))
                    for i in range(2)]
            slr = [Res(pb.snap_compute()), Res(pb.snap_compute())]
            mA_sb = sub.enter_context(nc.sbuf_tensor("sb_mA", [128, 512], BF16))
            mB_sb = sub.enter_context(nc.sbuf_tensor("sb_mB", [128, 512], BF16))
            bseed = pb.snap_compute()
            btok = [wload(mA_sb.ap()[:], cd["maskA"], extra_deps=bseed),
                    wload(mB_sb.ap()[:], cd["maskB"], extra_deps=bseed)]
            pb.ops.append(("vector", "FENCE", tuple(btok), None, None))
            tones = pb.op("vector", lambda E: E.memset(Vt.ap()[:, :, :, 64:65], 1.0))

            def proj(wnm, bias_sb, dst, hb, dst_off, to_vt):
                with ExitStack() as ps:
                    bid = f"bp{pb.uid()}_"
                    pps = [ps.enter_context(nc.psum_tensor(bid + f"pp{i}", [128, 1024], F32))
                           for i in range(2)]
                    ppr = [Res(), Res()]
                    vtr = [ps.enter_context(nc.psum_tensor(bid + f"vt{i}", [128, 128], BF16))
                           for i in range(2)]
                    vtrr = [Res(), Res()]
                    vfm = [ps.enter_context(nc.sbuf_tensor(bid + f"vf{i}", [128, 1024], BF16))
                           for i in range(2)]
                    vfr = [Res(), Res()]
                    for f in range(8):
                        i2 = f % 2
                        sl, sres = slab[i2], slr[i2]
                        ltok = wload(
                            sl.ap()[:], wscr[wnm][:, f, :, :],
                            extra_deps=sres.wdeps() + list(stage_end[wnm]))
                        sres.wrote(ltok)
                        pp, pres = pps[i2], ppr[i2]
                        tmm = None
                        mmdeps = pres.wdeps() + [ltok]
                        for k in range(8):
                            for tt in range(2):
                                tmm = pb.op("tensor",
                                            lambda E, o=pp.ap()[:, tt * 512:(tt + 1) * 512],
                                            l=sl.ap()[:, k, :],
                                            r=xn.ap()[:, k, tt * 512:(tt + 1) * 512],
                                            st_=(k == 0), sp=(k == 7):
                                            E.matmul(o, l, r, start=st_, stop=sp),
                                            deps=mmdeps if (k == 0 and tt == 0) else [])
                        pres.wrote(tmm)
                        sres.read(tmm)
                        if not to_vt:
                            tev = pb.op("scalar",
                                        lambda E, o=dst.ap()[:, f, dst_off:dst_off + 1024],
                                        i=pp.ap()[:], b=bias_sb.ap()[:, f:f + 1]:
                                        E.activation(o, i, AF.Identity, bias=b), deps=[tmm])
                            pres.read(tev)
                        else:
                            tev = pb.op("scalar", lambda E, o=vfm[i2].ap()[:],
                                        i=pp.ap()[:], b=bias_sb.ap()[:, f:f + 1]:
                                        E.activation(o, i, AF.Identity, bias=b),
                                        deps=[tmm] + vfr[i2].wdeps())
                            vfr[i2].wrote(tev)
                            pres.read(tev)
                            for tc in range(8):
                                j = hb // 128 + tc
                                j2 = tc % 2
                                ttk = pb.op("tensor", lambda E, o=vtr[j2].ap()[:],
                                            i=vfm[i2].ap()[:, tc * 128:(tc + 1) * 128],
                                            d=ident.ap()[:]: E.transpose(o, i, d),
                                            deps=[tev] + vtrr[j2].wdeps())
                                vtrr[j2].wrote(ttk)
                                vfr[i2].read(ttk)
                                tcp = pb.op("vector",
                                            lambda E,
                                            o=Vt.ap()[:, j, 2 * f:2 * f + 2, 0:64],
                                            i=vtr[j2].ap()[:].rearrange("p (a b) -> p a b", a=2):
                                            E.tensor_copy(o, i), deps=[ttk, tones])
                                vtrr[j2].read(tcp)
                    pb.fence(("tensor",))
                    pb.flush(bid)

            def mix(qh):
                with ExitStack() as ps:
                    bid = f"bm{qh}_{pb.uid()}_"
                    scp = [ps.enter_context(nc.psum_tensor(bid + f"sc{i}", [128, 1024], F32))
                           for i in range(2)]
                    scr = [Res(), Res()]
                    Ops = [ps.enter_context(nc.psum_tensor(bid + f"O{i}", [65, 1024], F32))
                           for i in range(2)]
                    Opr = [Res(), Res()]
                    expT = [ps.enter_context(nc.sbuf_tensor(bid + f"ex{i}", [128, 1024], BF16))
                            for i in range(2)]
                    expr = [Res(), Res()]
                    Oun = [ps.enter_context(nc.sbuf_tensor(bid + "ou", [65, 1024], BF16))] * 2
                    Our = [Res()] * 2
                    bcs = [ps.enter_context(nc.sbuf_tensor(bid + "bs", [64, 1024], F32))] * 2
                    bcsr = [Res()] * 2
                    dbf = [ps.enter_context(nc.sbuf_tensor(bid + "db", [1, 1024], BF16))] * 2
                    dbr = [Res()] * 2

                    steps = [(p, j, tt) for p in range(8) for j in range(16)
                             for tt in range(2)]
                    exp_tok = {}

                    def emit_scores(p, j, tt):
                        fch = p
                        i2 = (j * 2 + tt) % 2
                        sc, scres = scp[i2], scr[i2]
                        sdeps = scres.wdeps()
                        t_s = None
                        for hh in range(2):
                            t_s = pb.op("tensor",
                                        lambda E,
                                        o=sc.ap()[:, hh * 512:(hh + 1) * 512],
                                        l=Kf.ap()[hh * 64:hh * 64 + 64, fch, j * 128:(j + 1) * 128],
                                        r=Qf.ap()[hh * 64:hh * 64 + 64, fch, tt * 512:(tt + 1) * 512]:
                                        E.matmul(o, l, r, start=True, stop=True),
                                        deps=sdeps if hh == 0 else [])
                        scres.wrote(t_s)
                        t_e = pb.op("scalar", lambda E, o=expT[i2].ap()[:],
                                    i=sc.ap()[:]:
                                    E.activation(o, i, AF.Exp, scale=0.125),
                                    deps=[t_s] + expr[i2].wdeps())
                        expr[i2].wrote(t_e)
                        scres.read(t_e)
                        toks = [t_e, t_e]
                        if qh == 0 and j == 15 and tt == 0:
                            for hh in range(2):
                                tm = pb.op("vector",
                                           lambda E, o=expT[i2].ap()[:, hh * 512:(hh + 1) * 512],
                                           a=expT[i2].ap()[:, hh * 512:(hh + 1) * 512],
                                           b=mA_sb.ap()[:]:
                                           E.tensor_tensor(o, a, b, ALU.mult),
                                           deps=[t_e])
                                toks[hh] = tm
                                expr[i2].wrote(tm)
                        if qh == 1 and j == 0 and tt == 1:
                            for hh in range(2):
                                tm = pb.op("vector",
                                           lambda E, o=expT[i2].ap()[:, hh * 512:(hh + 1) * 512],
                                           a=expT[i2].ap()[:, hh * 512:(hh + 1) * 512],
                                           b=mB_sb.ap()[:]:
                                           E.tensor_tensor(o, a, b, ALU.mult),
                                           deps=[t_e])
                                toks[hh] = tm
                                expr[i2].wrote(tm)
                        exp_tok[(p, j, tt)] = toks

                    def emit_av(p, j, tt):
                        i2 = (j * 2 + tt) % 2
                        toks = exp_tok.pop((p, j, tt))
                        for hh in range(2):
                            h = 2 * p + hh
                            t_a = pb.op("tensor",
                                        lambda E, o=Ops[hh].ap()[:, tt * 512:(tt + 1) * 512],
                                        l=Vt.ap()[:, j, h, :],
                                        r=expT[i2].ap()[:, hh * 512:(hh + 1) * 512],
                                        st_=(j == 0), sp=(j == 15):
                                        E.matmul(o, l, r, start=st_, stop=sp),
                                        deps=[toks[hh]] + (Opr[hh].wdeps() if (j == 0 and tt == 0) else []))
                            expr[i2].read(t_a)
                            if j == 15 and tt == 1:
                                emit_epilogue(2 * p + hh, hh, t_a)

                    def emit_epilogue(h, hp, t_a):
                        pbase = (h % 2) * 64
                        fch = h // 2
                        evdeps = [t_a] + Our[hp].wdeps()
                        t_ev0 = pb.op("vector", lambda E, o=Oun[hp].ap()[0:64, :],
                                      i=Ops[hp].ap()[0:64, :]: E.tensor_copy(o, i),
                                      deps=evdeps)
                        t_ev = pb.op("vector", lambda E, o=Oun[hp].ap()[64:65, :],
                                     i=Ops[hp].ap()[64:65, :]: E.tensor_copy(o, i),
                                     deps=[t_a, t_ev0])
                        Our[hp].w = t_ev
                        Our[hp].r = [t_ev0]
                        t_bcv = None
                        for tt in range(2):
                            t_bc = pb.op("tensor",
                                         lambda E, o=Ops[hp].ap()[0:64, tt * 512:(tt + 1) * 512],
                                         l=ones65.ap()[64:65, :],
                                         r=Oun[hp].ap()[64:65, tt * 512:(tt + 1) * 512]:
                                         E.matmul(o, l, r, start=True, stop=True),
                                         deps=[t_ev, t_ev0])
                            Our[hp].read(t_bc)
                            t_bcv = pb.op("scalar",
                                          lambda E, o=bcs[hp].ap()[:, tt * 512:(tt + 1) * 512],
                                          i=Ops[hp].ap()[0:64, tt * 512:(tt + 1) * 512]:
                                          E.activation(o, i, AF.Ln),
                                          deps=[t_bc] + (bcsr[hp].wdeps() if tt == 0 else []))
                        bcsr[hp].wrote(t_bcv)
                        Opr[hp].w = t_bcv
                        Opr[hp].r = []
                        t_rc = pb.op("scalar", lambda E, o=bcs[hp].ap()[:],
                                     i=bcs[hp].ap()[:]:
                                     E.activation(o, i, AF.Exp, scale=-1.0),
                                     deps=[t_bcv])
                        bcsr[hp].wrote(t_rc)
                        t_on = pb.op("vector",
                                     lambda E, o=Of.ap()[pbase:pbase + 64, fch, :],
                                     a=Oun[hp].ap()[0:64, :], b=bcs[hp].ap()[:]:
                                     E.tensor_tensor(o, a, b, ALU.mult),
                                     deps=[t_rc])
                        Our[hp].read(t_on)
                        bcsr[hp].read(t_on)

                    for idx, stp in enumerate(steps):
                        emit_scores(*stp)
                        if idx > 0:
                            emit_av(*steps[idx - 1])
                    emit_av(*steps[-1])
                    pb.fence(("tensor",))
                    pb.flush(bid)

            def wo_half(qh):
                qoff = qh * 1024
                with ExitStack() as ps:
                    bid = f"bw{qh}_{pb.uid()}_"
                    wops = [ps.enter_context(nc.psum_tensor(bid + f"w{i}", [128, 512], F32))
                            for i in range(2)]
                    wopr = [Res(), Res()]
                    for f in range(8):
                        i2 = f % 2
                        sl, sres = slab[i2], slr[i2]
                        ltok = wload(
                            sl.ap()[:], wscr["bWo"][:, f, :, :],
                            extra_deps=sres.wdeps() + list(stage_end["bWo"]))
                        sres.wrote(ltok)
                        for tt in range(2):
                            wp, wres = wops[(f * 2 + tt) % 2], wopr[(f * 2 + tt) % 2]
                            tmm = None
                            mmdeps = wres.wdeps() + [ltok]
                            for k in range(8):
                                tmm = pb.op("tensor",
                                            lambda E, o=wp.ap()[:],
                                            l=sl.ap()[:, k, :],
                                            r=Of.ap()[:, k, tt * 512:(tt + 1) * 512],
                                            st_=(k == 0), sp=(k == 7):
                                            E.matmul(o, l, r, start=st_, stop=sp),
                                            deps=mmdeps if k == 0 else [])
                            wres.wrote(tmm)
                            sres.read(tmm)
                            lo = qoff + tt * 512
                            tres = pb.op("vector",
                                         lambda E, o=x_sb.ap()[:, f, lo:lo + 512],
                                         a=wp.ap()[:], b=x_sb.ap()[:, f, lo:lo + 512]:
                                         E.scalar_tensor_tensor(o, a, 1.0, b,
                                                                ALU.mult, ALU.add),
                                         deps=[tmm])
                            wres.read(tres)
                    pb.fence(("tensor",))
                    pb.flush(bid)

            emit_ln(0, 1024)
            proj("bWk", bk_sb, Kf, 0, 0, False)
            proj("bWv", bv_sb, None, 0, 0, True)
            proj("bWq", bq_sb, Qf, 0, 0, False)
            emit_ln(1024, 1024)
            proj("bWk", bk_sb, Kf, 1024, 1024, False)
            proj("bWv", bv_sb, None, 1024, 0, True)
            mix(0)
            wo_half(0)
            proj("bWq", bq_sb, Qf, 1024, 0, False)
            mix(1)
            wo_half(1)
            pb.fence()
            pb.flush("endb")

    subs = [emit_attn_a, emit_ffn, emit_attn_b, emit_ffn]
    for i in range(n_sub):
        subs[i]()

    pb.fence()
    if n_sub in (2, 4):
        tst = None
        for f in range(8):
            tst = pb.dma("sync", y_out[:, f, :], x_sb.ap()[:, f, :], sem_io,
                         deps=[dn_last[(0, f)], dn_last[(1024, f)]])
        pb.op("sync", lambda E: E.nop(), deps=[tst])
    else:
        tst = pb.dma("sync", y_out, x_sb.ap()[:], sem_io, deps=pb.snap_compute())
        pb.op("sync", lambda E: E.nop(), deps=[tst])
    pb.final_flush("prog")
    return nc


_NC_CACHE = {}


def _get_nc(n_sub):
    if n_sub not in _NC_CACHE:
        _NC_CACHE[n_sub] = build_program(n_sub)
    return _NC_CACHE[n_sub]


def kernel(L0_attn_a_hidden_states, L0_attn_a_Wq, L0_attn_a_Wk, L0_attn_a_Wv,
           L0_attn_a_Wg, L0_attn_a_Wo, L0_attn_a_g_norm_w,
           L0_attn_b_Wq, L0_attn_b_bq, L0_attn_b_Wk, L0_attn_b_bk,
           L0_attn_b_Wv, L0_attn_b_bv, L0_attn_b_Wo,
           L0_ffn_w_gate, L0_ffn_w_up, L0_ffn_w_down):
    return _kernel_impl(L0_attn_a_hidden_states, L0_attn_a_Wq, L0_attn_a_Wk,
                        L0_attn_a_Wv, L0_attn_a_Wg, L0_attn_a_Wo, L0_attn_a_g_norm_w,
                        L0_attn_b_Wq, L0_attn_b_bq, L0_attn_b_Wk, L0_attn_b_bk,
                        L0_attn_b_Wv, L0_attn_b_bv, L0_attn_b_Wo,
                        L0_ffn_w_gate, L0_ffn_w_up, L0_ffn_w_down)


def host_in_maps(L0_attn_a_hidden_states, L0_attn_a_Wq, L0_attn_a_Wk, L0_attn_a_Wv,
                 L0_attn_a_Wg, L0_attn_a_Wo, L0_attn_a_g_norm_w,
                 L0_attn_b_Wq, L0_attn_b_bq, L0_attn_b_Wk, L0_attn_b_bk,
                 L0_attn_b_Wv, L0_attn_b_bv, L0_attn_b_Wo,
                 L0_ffn_w_gate, L0_ffn_w_up, L0_ffn_w_down):
    x = np.asarray(L0_attn_a_hidden_states, dtype=np.float32)
    B = x.shape[0]
    com = {
        "aWq": np.ascontiguousarray(np.asarray(L0_attn_a_Wq, np.float32)),
        "aWk": np.ascontiguousarray(np.asarray(L0_attn_a_Wk, np.float32)),
        "aWv": np.ascontiguousarray(np.asarray(L0_attn_a_Wv, np.float32)),
        "aWg": np.ascontiguousarray(np.asarray(L0_attn_a_Wg, np.float32)),
        "aWo": np.ascontiguousarray(np.asarray(L0_attn_a_Wo, np.float32)),
        "bWq": np.ascontiguousarray(np.asarray(L0_attn_b_Wq, np.float32).T),
        "bWk": np.ascontiguousarray(np.asarray(L0_attn_b_Wk, np.float32).T),
        "bWv": np.ascontiguousarray(np.asarray(L0_attn_b_Wv, np.float32).T),
        "bWo": np.ascontiguousarray(np.asarray(L0_attn_b_Wo, np.float32).T),
        "g_fm": np.ascontiguousarray(
            np.asarray(L0_attn_a_g_norm_w, np.float32).reshape(2, 128).T),
        "bq_fm": np.ascontiguousarray(np.asarray(L0_attn_b_bq, np.float32).reshape(8, 128).T),
        "bk_fm": np.ascontiguousarray(np.asarray(L0_attn_b_bk, np.float32).reshape(8, 128).T),
        "bv_fm": np.ascontiguousarray(np.asarray(L0_attn_b_bv, np.float32).reshape(8, 128).T),
        "fgT": np.ascontiguousarray(np.asarray(L0_ffn_w_gate, np.int32).T),
        "fuT": np.ascontiguousarray(np.asarray(L0_ffn_w_up, np.int32).T),
        "fdT": np.ascontiguousarray(np.asarray(L0_ffn_w_down, np.int32).T),
    }
    in_maps = []
    for b in range(B):
        m = dict(com)
        xT = np.ascontiguousarray(x[b].T)
        m["x_in"] = np.ascontiguousarray(xT.reshape(8, 128, T).transpose(1, 0, 2))
        in_maps.append(m)
    return in_maps


def _kernel_impl(*args):
    in_maps = host_in_maps(*args)
    B = len(in_maps)
    nc = _get_nc(N_SUB)
    res = run_bass_kernel_spmd(nc, in_maps, core_ids=list(range(B)))
    outs = []
    for r in res.results:
        y = r["y_out"]
        yd = y.transpose(1, 0, 2).reshape(D, T)
        outs.append(np.ascontiguousarray(yd.T))
    return np.stack(outs, axis=0)
